# version 12
# baseline (speedup 1.0000x reference)
import numpy as np
import ml_dtypes
import concourse.bass as bass
import concourse.mybir as mybir
from concourse.bass_utils import run_bass_kernel_spmd

F32 = mybir.dt.float32
BF16 = mybir.dt.bfloat16
AF = mybir.ActivationFunctionType
ALU = mybir.AluOpType
AX = mybir.AxisListType
BF = ml_dtypes.bfloat16

import os
DEV_SKIP = set(os.environ.get("KDEV_SKIP", "").split(","))

B, S, D = 2, 8192, 1024
NCORE = 8
OWN = 2048
CH = 64
WIN = 512
HID = 2730
HIDP = 2816
F1 = 5632
NEG = -1.0e30

R2, R1, R0E = 3328, 2688, 2176
T2, T1, T0E = 26, 21, 17
XC = R2 + S
QROW0 = R2
KT = S // 128
SLAB = 8


def _bf(a):
    return np.ascontiguousarray(np.asarray(a)).astype(BF)



class Sched:
    def __init__(self, nc):
        self.nc = nc
        self.ops = []
        self.sems = {}

    def _sem(self, name):
        if name not in self.sems:
            self.sems[name] = [self.nc.alloc_semaphore(name), 0]
        return self.sems[name]

    def op(self, eng, fn, waits=(), tick=None):
        tk = None
        if tick is not None:
            s = self._sem(tick)
            amt = 16 if tick.startswith("d_") else 1
            s[1] += amt
            tk = (tick, s[1] if False else amt)
        ws = []
        for w in waits:
            if w is None:
                continue
            if isinstance(w, list):
                ws.extend(x for x in w if x is not None)
            else:
                ws.append(w)
        self.ops.append((eng, fn, ws, (tick, tk[1]) if tk else None))
        return (tick, self.sems[tick][1]) if tick else None

    def count(self, name):
        return (name, self.sems[name][1]) if name in self.sems else None

    def barrier(self, engines):
        marks = {}
        for e in engines:
            nm = "bar_" + e
            s = self._sem(nm)
            s[1] += 1
            marks[e] = s[1]
            self.ops.append((e, "NOP", [], (nm, 1)))
        comp = [self.count(nm) for nm in
                ("pe", "dve", "act", "gpc", "d_sp", "d_gp")]
        comp = [c for c in comp if c is not None]
        for e in engines:
            self.ops.append((e, None,
                             [("bar_" + o, marks[o]) for o in engines if o != e]
                             + comp, None))

    def replay(self, blk, engines):
        allwait = "SELFWAIT" in DEV_SKIP
        selfwait_eng = {"vector", "scalar"}
        own_sems = {"sync": ["d_sp"], "gpsimd": ["d_gp", "gpc"],
                    "tensor": ["pe"], "vector": ["dve"], "scalar": ["act"]}

        def engine_body(ename):
            def body(e):
                waited = {}
                counts = {}
                for (eng, fn, waits, tk) in self.ops:
                    if eng != ename:
                        continue
                    if allwait or ename in selfwait_eng:
                        for sn in own_sems[ename]:
                            c = counts.get(sn, 0)
                            if c > waited.get(sn, 0) and sn in self.sems:
                                e.wait_ge(self.sems[sn][0], c)
                                waited[sn] = c
                    for (sn, val) in waits:
                        if waited.get(sn, 0) >= val:
                            continue
                        e.wait_ge(self.sems[sn][0], val)
                        waited[sn] = val
                    if fn == "NOP":
                        instr = e.engine_nop() if hasattr(e, "engine_nop") else e.nop()
                    elif fn is not None:
                        instr = fn(e)
                    else:
                        instr = None
                    if tk is not None and instr is not None:
                        instr.then_inc(self.sems[tk[0]][0], tk[1])
                        counts[tk[0]] = counts.get(tk[0], 0) + tk[1]
                    elif tk is not None:
                        raise RuntimeError("tick with no instruction")
            return body
        for ename in engines:
            getattr(blk, ename)(engine_body(ename))


class Ring:

    def __init__(self, n):
        self.n = n
        self.t = [[] for _ in range(n)]
        self.gen = [None] * n

    def wait(self, i):
        return self.t[i % self.n]

    def done(self, i, tick):
        s = i % self.n
        if self.gen[s] != i:
            self.t[s] = []
            self.gen[s] = i
        if tick is not None:
            self.t[s] = self.t[s] + [tick]


class View:

    def __init__(self, big, off_bytes, shape, dtype):
        self.big = big
        self.off = off_bytes
        self.shape = shape
        self.dtype = dtype
        self.esz = mybir.dt.size(dtype)

    def ap(self):
        a = self.big.ap().bitcast(self.dtype)
        a = a[0:self.shape[0], self.off // self.esz: self.off // self.esz + self.shape[1]]
        return a

    def __getitem__(self, idx):
        return self.ap()[idx]


class Arena:

    def __init__(self, nc, limit=206000):
        self.nc = nc
        self.off = 0
        self.top = limit
        self.limit = limit
        self.big = nc.alloc_sbuf_tensor("arena", [128, limit], mybir.dt.uint8)

    registry = {}

    def alloc(self, name, shape, dtype, align=32):
        per_part = int(np.prod(shape[1:])) * mybir.dt.size(dtype)
        self.off = (self.off + align - 1) // align * align
        v = View(self.big, self.off, [shape[0], int(np.prod(shape[1:]))], dtype)
        Arena.registry[name] = (self.off, shape, dtype)
        self.off += per_part
        assert self.off <= self.top, f"SBUF overflow at {name}: {self.off} > top {self.top}"
        return v

    def alloc_top(self, name, shape, dtype, align=32):
        per_part = int(np.prod(shape[1:])) * mybir.dt.size(dtype)
        newtop = (self.top - per_part) // align * align
        assert newtop >= self.off, f"SBUF overflow(top) at {name}"
        self.top = newtop
        Arena.registry[name] = (newtop, shape, dtype)
        return View(self.big, self.top, [shape[0], int(np.prod(shape[1:]))], dtype)

    def mark(self):
        return self.off

    def reset(self, mark):
        self.off = mark

    def mark_top(self):
        return self.top

    def reset_top(self, mark):
        self.top = mark

    def finalize(self):
        pass



def _prep_shared(inp):
    w = {}
    w["ident_bf"] = _bf(np.eye(128, dtype=np.float32))
    w["ident_f32"] = np.eye(128, dtype=np.float32)
    w["ones_bf"] = _bf(np.ones((128, 128), np.float32))
    kr = np.arange(128)[:, None]
    qr = np.arange(128)[None, :]
    tri = np.zeros((128, 2, 128), np.float32)
    d0 = qr - kr + 512
    d4 = qr - kr
    tri[:, 0, :] = np.where((d0 >= 0) & (d0 <= 512), 0.0, NEG)
    tri[:, 1, :] = np.where((d4 >= 0) & (d4 <= 512), 0.0, NEG)
    w["triT"] = _bf(np.ascontiguousarray(tri.reshape(128, 256)))
    w["ident_m8"] = np.eye(128, dtype=np.float32) * -8.0

    Wqkv = np.asarray(inp["L0_attn_W_qkv"], np.float32)
    qrows = np.concatenate([np.arange(h * 192, h * 192 + 64) for h in range(16)])
    w["Wq_t"] = np.ascontiguousarray(Wqkv[qrows].T)
    w["Wk_t"] = np.ascontiguousarray(Wqkv[qrows + 64].T)
    w["Wv_t"] = np.ascontiguousarray(Wqkv[qrows + 128].T)
    w["Wout_t"] = np.ascontiguousarray(np.asarray(inp["L0_attn_W_out"], np.float32).T)
    w["b_out"] = np.ascontiguousarray(np.asarray(inp["L0_attn_b_out"], np.float32)[None, :])
    for l in range(2):
        w[f"wqkv{l}"] = np.ascontiguousarray(np.asarray(inp["L0_ffn_wqkv"], np.float32)[l])
        w[f"wo{l}"] = np.ascontiguousarray(np.asarray(inp["L0_ffn_wo"], np.float32)[l])
        w1 = np.asarray(inp["L0_ffn_w1"], np.float32)[l]
        b1 = np.asarray(inp["L0_ffn_b1"], np.float32)[l]
        w2 = np.asarray(inp["L0_ffn_w2"], np.float32)[l]
        w1p = np.zeros((D, F1), np.float32)
        b1p = np.zeros((F1,), np.float32)
        for i in range(HIDP // 128):
            if i * 128 < HID:
                n = min(128, HID - i * 128)
                w1p[:, 2 * i * 128:2 * i * 128 + n] = w1[:, i * 128:i * 128 + n]
                w1p[:, (2 * i + 1) * 128:(2 * i + 1) * 128 + n] = w1[:, HID + i * 128:HID + i * 128 + n]
                b1p[2 * i * 128:2 * i * 128 + n] = b1[i * 128:i * 128 + n]
                b1p[(2 * i + 1) * 128:(2 * i + 1) * 128 + n] = b1[HID + i * 128:HID + i * 128 + n]
        w2p = np.zeros((HIDP, D), np.float32)
        w2p[:HID] = w2
        w[f"w1p{l}"] = np.ascontiguousarray(w1p)
        w[f"b1p{l}"] = np.ascontiguousarray(b1p.reshape(F1 // 128, 128).T)
        w[f"w2p{l}"] = np.ascontiguousarray(w2p)
        w[f"b2_{l}"] = np.ascontiguousarray(np.asarray(inp["L0_ffn_b2"], np.float32)[l][None, :])
        w[f"lng_row{l}"] = _bf(np.asarray(inp["L0_ffn_ln_g"], np.float32)[l][None, :])
        w[f"lnb_col{l}"] = np.ascontiguousarray(
            np.asarray(inp["L0_ffn_ln_b"], np.float32)[l].reshape(8, 128).T)
        w[f"qs{l}"] = np.ascontiguousarray(np.asarray(inp["L0_ffn_q_scale"], np.float32)[l][None, :])
        w[f"ks{l}"] = np.ascontiguousarray(np.asarray(inp["L0_ffn_k_scale"], np.float32)[l][None, :])
    return w


def _prep_core(x, b, j):
    s0 = j * OWN
    xb = np.asarray(x[b], np.float32)
    r2abs = s0 - 1280
    sect_f = np.zeros((R2, D), np.float32)
    lo = max(0, r2abs)
    sect_f[lo - r2abs:, :] = xb[lo:s0 + OWN]
    xcore = np.concatenate([sect_f, np.roll(xb, -s0, axis=0)], axis=0)

    t = np.arange(r2abs, r2abs + R2, dtype=np.float64)
    half = np.arange(0, CH, 2, dtype=np.float64)
    inv = 1.0 / 10000.0 ** (half / CH)
    fr32 = (t[:, None].astype(np.float32) * inv[None, :].astype(np.float32))
    freqs = np.concatenate([fr32, fr32], axis=1).astype(np.float64)
    base = (half + 0.4 * CH) / (1.4 * CH)
    power = (t - s0) / float(WIN // 2)
    scale = base[None, :] ** power[:, None]
    scale = np.concatenate([scale, scale], axis=1)
    cos, sin = np.cos(freqs), np.sin(freqs)
    sgn = np.concatenate([-np.ones(32), np.ones(32)])[None, :]
    valid = (t >= 0).astype(np.float32)
    return {
        "xcore": xcore,
        "cq": (cos * scale).astype(np.float32),
        "sq": (sin * scale * sgn).astype(np.float32),
        "ck": (cos / scale).astype(np.float32),
        "sk": (sin / scale * sgn).astype(np.float32),
        "vld": np.ascontiguousarray(valid.reshape(T2, 128).T),
    }



def build_program():
    nc = bass.Bass("TRN2", target_bir_lowering=False, debug=False)
    ar = Arena(nc)
    S_ = Sched(nc)
    ENG = ["sync", "gpsimd", "tensor", "vector", "scalar"]

    dr = {}
    def din(name, shape, dtype=F32):
        dr[name] = nc.dram_tensor(name, list(shape), dtype, kind="ExternalInput")
        return dr[name]

    din("xcore", [XC, D])
    din("Wq_t", [D, D]); din("Wk_t", [D, D]); din("Wv_t", [D, D])
    din("Wout_t", [D, D]); din("b_out", [1, D])
    for l in range(2):
        din(f"wqkv{l}", [D, 1536]); din(f"wo{l}", [512, D])
        din(f"w1p{l}", [D, F1]); din(f"b1p{l}", [128, F1 // 128])
        din(f"w2p{l}", [HIDP, D]); din(f"b2_{l}", [1, D])
        din(f"lng_row{l}", [1, D], BF16); din(f"lnb_col{l}", [128, 8])
        din(f"qs{l}", [1, CH]); din(f"ks{l}", [1, CH])
    din("cq", [R2, CH]); din("sq", [R2, CH]); din("ck", [R2, CH]); din("sk", [R2, CH])
    din("vld", [128, T2])
    din("ident_bf", [128, 128], BF16); din("ident_f32", [128, 128])
    din("ones_bf", [128, 128], BF16); din("triT", [128, 256], BF16)
    din("ident_m8", [128, 128])

    out = nc.dram_tensor("out", [OWN, D], F32, kind="ExternalOutput")

    k_scr = nc.dram_tensor("k_scr", [D, S], BF16, kind="Internal")
    v_scr = nc.dram_tensor("v_scr", [8, 128, KT * 130], BF16, kind="Internal")
    xn_scr = nc.dram_tensor("xn_scr", [R2, D], F32, kind="Internal")
    outp_scr = nc.dram_tensor("outp_scr", [OWN, D], F32, kind="Internal")
    a_scr = nc.dram_tensor("a_scr", [T1 * 128, D], F32, kind="Internal")
    x1_scr = nc.dram_tensor("x1_scr", [T1 * 128, D], F32, kind="Internal")
    x2_scr = nc.dram_tensor("x2_scr", [T1 * 128, D], F32, kind="Internal")
    if "TAP2" in DEV_SKIP:
        tap_wo = nc.dram_tensor("tap_wo", [2, 128, 4 * D], BF16, kind="Internal")
        tap_q = nc.dram_tensor("tap_q", [128, 4 * T2 * 128], BF16, kind="Internal")
        tap_k = nc.dram_tensor("tap_k", [128, 4 * T2 * 128], BF16, kind="Internal")
        tap_v = nc.dram_tensor("tap_v", [128, T2 * 520], BF16, kind="Internal")
        tap_o = nc.dram_tensor("tap_o", [128, 4 * T1 * 128], BF16, kind="Internal")
    if "TAP" in DEV_SKIP:
        tap_scr = nc.dram_tensor("tap_scr", [16, 128, 4 * D], BF16, kind="Internal")
        tap0_scr = nc.dram_tensor("tap0_scr", [16, 128, D], BF16, kind="Internal")
        tapK_scr = nc.dram_tensor("tapK_scr", [16, 128, D], BF16, kind="Internal")
        tapV_scr = nc.dram_tensor("tapV_scr", [16, 128, D], BF16, kind="Internal")
    hq_scr = nc.dram_tensor("hq_scr", [8, 128, T1 * 128], BF16, kind="Internal")

    pbig = [nc.alloc_psum_tensor(f"pbig{i}", [128, 1024], F32) for i in range(2)]
    pb = [pbig[0][:, 0:512], pbig[0][:, 512:1024],
          pbig[1][:, 0:512], pbig[1][:, 512:1024]] + \
         [nc.alloc_psum_tensor(f"pb{i}", [128, 512], F32) for i in range(4, 6)]
    pbt = [nc.alloc_psum_tensor(f"pbt{i}", [128, 1024], BF16) for i in range(2)]

    s_id_bf = ar.alloc("s_id_bf", [128, 128], BF16)
    s_id_f = ar.alloc("s_id_f", [128, 128], F32)
    s_ones = ar.alloc("s_ones", [128, 128], BF16)
    s_tri = ar.alloc("s_tri", [128, 256], BF16)
    s_idm8 = ar.alloc("s_idm8", [128, 128], F32)
    s_vld = ar.alloc("s_vld", [128, T2], F32)
    dmp = ar.alloc("dmpG", [128, D], F32)

    def OP(eng, fn, waits=(), tick=None):
        return S_.op(eng, fn, waits, tick)

    def load(dst, src, q="sync", slow=False):
        t = "d_gp" if q == "gpsimd" else "d_sp"
        if slow:
            return OP(q, lambda e, d=dst, s=src: e.dma_start(
                d, s, allow_slow_non_contiguous=True), tick=t)
        return OP(q, lambda e, d=dst, s=src: e.dma_start(d, s), tick=t)

    c_id = load(s_id_bf[:], dr["ident_bf"][:])
    c_idf = load(s_id_f[:], dr["ident_f32"][:])
    c_ones = load(s_ones[:], dr["ones_bf"][:])
    c_tri = load(s_tri[:], dr["triT"][:])
    c_im8 = load(s_idm8[:], dr["ident_m8"][:])
    c_vld = load(s_vld[:], dr["vld"][:])

    bcast_guard = [None]

    def bcast_row_to_tile(row_ap, dst_ap, ncols, waits, bank, lhs=None):
        lh = lhs if lhs is not None else s_ones[0:1, :]
        t1 = OP("tensor", lambda e, lh=lh, r=row_ap, b=bank, n=ncols: e.matmul(
            pb[b][:, 0:n], lh, r, start=True, stop=True),
            waits=list(waits) + [c_ones, bcast_guard[0]], tick="pe")
        t2 = OP("vector", lambda e, d=dst_ap, b=bank, n=ncols: e.tensor_copy(
            d, pb[b][:, 0:n]), waits=[t1], tick="dve")
        bcast_guard[0] = t2
        return t2

    topX = ar.mark_top()
    XST = R2 + 8
    xnff = ar.alloc_top("xnff", [128, 8 * XST], BF16)
    topQ = ar.mark_top()
    q_fm = ar.alloc_top("q_fm", [128, 8 * OWN], BF16)

    mA = ar.mark()
    xt = [ar.alloc(f"xt{i}", [128, D], F32) for i in range(3)]
    xnb = [ar.alloc(f"xnb{i}", [128, D], BF16) for i in range(2)]
    xnf32 = [ar.alloc(f"xnf32_{i}", [128, D], F32) for i in range(2)]
    st = [ar.alloc(f"stA{i}", [128, 8], F32) for i in range(2)]
    kstg = [ar.alloc(f"kstg{i}", [128, 512], BF16) for i in range(2)]
    vstg = [ar.alloc(f"vstg{i}", [128, 16 * 65], BF16) for i in range(2)]
    xq4 = [ar.alloc(f"xq4_{i}", [128, 4 * D], BF16) for i in range(2)]
    wA = ar.alloc("wA", [128, 8 * 3 * D], BF16)

    for ci in range(8):
        for wi, nm in enumerate(["Wq_t", "Wk_t", "Wv_t"]):
            load(wA[:, (ci * 3 + wi) * D:(ci * 3 + wi + 1) * D],
                 dr[nm][ci * 128:(ci + 1) * 128, :], q="gpsimd")
    wA_all = S_.count("d_gp")

    OP("vector", lambda e: e.memset(
        xnff.ap().rearrange("p (c t) -> p c t", c=8, t=XST)[:, :, 0:1], 0.0),
       tick="dve")

    r_xt = Ring(3)
    r_xnb = Ring(2)
    r_xnf32 = Ring(2)
    r_st = Ring(2)
    r_pbt = Ring(2)
    r_kbank = Ring(2)
    r_vbank = Ring(2)
    r_qbank = Ring(2)
    r_kstg = Ring(2)
    r_vstg = Ring(2)

    def ln_tile(idx, row, want_f32):
        xs = idx % 3
        ss = st[idx % 2][:]
        t_ld = OP("sync", lambda e, xs=xs, row=row: e.dma_start(
            xt[xs][:], dr["xcore"][row:row + 128, :]),
            waits=[r_xt.wait(idx)], tick="d_sp")
        t_sum = OP("vector", lambda e, xs=xs, ss=ss: e.tensor_scalar(
            dmp[:], xt[xs][:], 1.0, None, ALU.mult, ALU.add,
            accum_out=ss[:, 0:1]), waits=[t_ld, r_st.wait(idx)], tick="dve")
        t_ssq = OP("scalar", lambda e, xs=xs, ss=ss, i=idx % 2: e.activation(
            xnb[i][:], xt[xs][:], AF.Square, accum_out=ss[:, 1:2]),
            waits=[t_ld, r_st.wait(idx), t_sum, r_xnb.wait(idx)], tick="act")
        OP("vector", lambda e, ss=ss: e.tensor_scalar(
            ss[:, 2:3], ss[:, 0:1], 1.0 / D, None, ALU.mult, ALU.bypass),
           waits=[t_ssq], tick="dve")
        OP("vector", lambda e, ss=ss: e.tensor_scalar(
            ss[:, 3:4], ss[:, 2:3], ss[:, 2:3], None, ALU.mult, ALU.bypass),
           tick="dve")
        OP("vector", lambda e, ss=ss: e.tensor_scalar(
            ss[:, 4:5], ss[:, 1:2], 1.0 / D, ss[:, 3:4], ALU.mult, ALU.subtract),
           tick="dve")
        t_v = OP("vector", lambda e, ss=ss: e.tensor_scalar(
            ss[:, 4:5], ss[:, 4:5], 0.0, 1e-5, ALU.max, ALU.add), tick="dve")
        t_sq = OP("scalar", lambda e, ss=ss: e.activation(
            ss[:, 5:6], ss[:, 4:5], AF.Sqrt), waits=[t_v], tick="act")
        t_r = OP("vector", lambda e, ss=ss: e.reciprocal(ss[:, 6:7], ss[:, 5:6]),
                 waits=[t_sq], tick="dve")
        if want_f32:
            t_f = OP("vector", lambda e, xs=xs, ss=ss, i=idx % 2: e.tensor_scalar(
                xnf32[i][:], xt[xs][:], ss[:, 2:3], ss[:, 6:7],
                ALU.subtract, ALU.mult),
                waits=[t_r, r_xnf32.wait(idx)], tick="dve")
            t_b = OP("scalar", lambda e, i=idx % 2: e.copy(xnb[idx % 2][:], xnf32[i][:]),
                     waits=[t_f, r_xnb.wait(idx)], tick="act")
        else:
            t_b = OP("vector", lambda e, xs=xs, ss=ss: e.tensor_scalar(
                xnb[idx % 2][:], xt[xs][:], ss[:, 2:3], ss[:, 6:7],
                ALU.subtract, ALU.mult),
                waits=[t_r, r_xnb.wait(idx)], tick="dve")
            t_f = None
        r_st.done(idx, t_b)
        r_xt.done(idx, t_b)
        return t_b, t_f

    def tr_tile(idx, src_bf_ap, dst_view, wait, dst_free=None):
        pt = pbt[idx % 2]
        for c in range(8):
            OP("tensor", lambda e, c=c, pt=pt, s=src_bf_ap: e.transpose(
                pt[:, c * 128:(c + 1) * 128],
                s[:, c * 128:(c + 1) * 128], s_id_bf[:]),
               waits=([wait, c_id, r_pbt.wait(idx)] if c == 0 else []),
               tick=("pe" if c == 7 else None))
        t_tr = S_.count("pe")
        t_cp = OP("vector", lambda e, d=dst_view, pt=pt: e.tensor_copy(
            d, pt[:, 0:1024].rearrange("p (c t) -> p c t", c=8)),
            waits=[t_tr, dst_free], tick="dve")
        r_pbt.done(idx, t_cp)
        return t_cp

    xnffv = xnff.ap().rearrange("p (c t) -> p c t", c=8, t=XST)

    for ti in range(T2):
        t_b, t_f = ln_tile(ti, ti * 128, want_f32=True)
        t_st = OP("sync", lambda e, ti=ti, i=ti % 2: e.dma_start(
            xn_scr[ti * 128:(ti + 1) * 128, :], xnf32[i][:]),
            waits=[t_f], tick="d_sp")
        r_xnf32.done(ti, t_st)
        t_cp = tr_tile(ti, xnb[ti % 2][:], xnffv[:, :, 1 + ti * 128:1 + (ti + 1) * 128], t_b)
        r_xnb.done(ti, t_cp)

    r_xq4 = Ring(2)
    for g in range(16):
        gs = 4 * g
        slot = g % 2
        xgv = xq4[slot].ap().rearrange("p (c t) -> p c t", c=8, t=512)
        t_last = None
        for i in range(4):
            ti = gs + i
            idx = T2 + ti
            t_b, _ = ln_tile(idx, R2 + ti * 128, want_f32=False)
            t_cp = tr_tile(idx, xnb[idx % 2][:], xgv[:, :, i * 128:(i + 1) * 128], t_b,
                           dst_free=r_xq4.wait(g))
            r_xnb.done(idx, t_cp)
            t_last = t_cp
            if "TAP" in DEV_SKIP and i == 0:
                OP("sync", lambda e, g=g, slot=slot: e.dma_start(
                    tap0_scr.ap()[g:g + 1, :, :].squeeze(0)
                    .rearrange("p (c t) -> p c t", c=8),
                    xq4[slot].ap().rearrange("p (c t) -> p c t", c=8, t=512)[:, :, 0:128]),
                   waits=[t_cp], tick="d_sp")
        for oc in range(8):
            bank = oc % 2
            for ci in range(8):
                OP("tensor", lambda e, oc=oc, ci=ci, bank=bank, xgv=xgv: e.matmul(
                    pb[bank][:, 0:512],
                    wA[:, (ci * 3 + 1) * D + oc * 128:(ci * 3 + 1) * D + oc * 128 + 128],
                    xgv[:, ci, :], start=(ci == 0), stop=(ci == 7)),
                   waits=([t_last, wA_all, r_kbank.wait(oc)] if ci == 0 else []),
                   tick=("pe" if ci == 7 else None))
            t_k = S_.count("pe")
            t_ev = OP("vector", lambda e, bank=bank, oc=oc: e.tensor_copy(
                kstg[oc % 2][:], pb[bank][:, 0:512]),
                waits=[t_k, r_kstg.wait(oc)], tick="dve")
            r_kbank.done(oc, t_ev)
            t_dk = OP("sync", lambda e, oc=oc, gs=gs: e.dma_start(
                k_scr[oc * 128:(oc + 1) * 128, gs * 128:gs * 128 + 512],
                kstg[oc % 2][:]), waits=[t_ev], tick="d_sp")
            r_kstg.done(oc, t_dk)
        if "TAP" in DEV_SKIP:
            OP("sync", lambda e, g=g, slot=slot: e.dma_start(
                tapK_scr.ap()[g:g + 1, :, :].squeeze(0)
                .rearrange("p (c t) -> p c t", c=8),
                xq4[slot].ap().rearrange("p (c t) -> p c t", c=8, t=512)[:, :, 0:128]),
               waits=[S_.count("d_sp"), S_.count("dve"), S_.count("pe")], tick="d_sp")
        for i in range(4 if "V" not in DEV_SKIP else 0):
            for half in range(2):
                bank = 2 + half
                for ci in range(8):
                    OP("tensor", lambda e, i=i, half=half, ci=ci, bank=bank, xgv=xgv:
                       e.matmul(
                        pb[bank][:, 0:512],
                        xgv[:, ci, i * 128:(i + 1) * 128],
                        wA[:, (ci * 3 + 2) * D + half * 512:
                           (ci * 3 + 2) * D + half * 512 + 512],
                        start=(ci == 0), stop=(ci == 7)),
                       waits=([r_vbank.wait(half)] if ci == 0 else []),
                       tick=("pe" if (ci == 7) else None))
            t_v = S_.count("pe")
            vs = vstg[i % 2].ap().rearrange("p (h c) -> p h c", h=16)
            OP("vector", lambda e, vs=vs: e.tensor_copy(
                vs[:, 0:8, 0:64], pb[2][:, 0:512].rearrange("p (h c) -> p h c", h=8)),
               waits=[t_v, r_vstg.wait(i)], tick="dve")
            t_e2 = OP("vector", lambda e, vs=vs: e.tensor_copy(
                vs[:, 8:16, 0:64], pb[3][:, 0:512].rearrange("p (h c) -> p h c", h=8)),
               tick="dve")
            r_vbank.done(0, t_e2)
            r_vbank.done(1, t_e2)
            t_vc = OP("vector", lambda e, vs=vs: e.memset(vs[:, :, 64:65], 1.0),
                      tick="dve")
            t_dv = None
            for p8 in range(8):
                t_dv = OP("sync", lambda e, p8=p8, ti=gs + i, i=i: e.dma_start(
                    v_scr[p8, :, ti * 130:(ti + 1) * 130],
                    vstg[i % 2][:, p8 * 130:(p8 + 1) * 130]),
                    waits=[t_vc], tick="d_sp")
            r_vstg.done(i, t_dv)
        if "TAP" in DEV_SKIP:
            OP("sync", lambda e, g=g, slot=slot: e.dma_start(
                tapV_scr.ap()[g:g + 1, :, :].squeeze(0)
                .rearrange("p (c t) -> p c t", c=8),
                xq4[slot].ap().rearrange("p (c t) -> p c t", c=8, t=512)[:, :, 0:128]),
               waits=[S_.count("d_sp"), S_.count("dve"), S_.count("pe")], tick="d_sp")
        if g < 4 and "Q" not in DEV_SKIP:
            qfv = q_fm.ap().rearrange("p (c t) -> p c t", c=8, t=OWN)
            for oc in range(8):
                bank = 4 + (oc % 2)
                for ci in range(8):
                    OP("tensor", lambda e, oc=oc, ci=ci, bank=bank, xgv=xgv: e.matmul(
                        pb[bank][:, 0:512],
                        wA[:, (ci * 3 + 0) * D + oc * 128:(ci * 3 + 0) * D + oc * 128 + 128],
                        xgv[:, ci, :], start=(ci == 0), stop=(ci == 7)),
                       waits=([r_qbank.wait(oc)] if ci == 0 else []),
                       tick=("pe" if ci == 7 else None))
                t_q = S_.count("pe")
                t_eq = OP("vector", lambda e, bank=bank, oc=oc, gs=gs, qfv=qfv:
                          e.tensor_copy(
                              qfv[:, oc, gs * 128:gs * 128 + 512], pb[bank][:, 0:512]),
                          waits=[t_q], tick="dve")
                r_qbank.done(oc, t_eq)
        if "TAP" in DEV_SKIP:
            OP("sync", lambda e, g=g, slot=slot: e.dma_start(
                tap_scr[g, :, :], xq4[slot][:]), waits=[t_last], tick="d_sp")
        r_xq4.done(g, S_.count("pe"))

    S_.barrier(ENG)
    ar.reset(mA)

    mB = ar.mark()
    kbuf = [ar.alloc(f"kbuf{i}", [128, S], BF16) for i in range(2)]
    vbuf = [ar.alloc(f"vbuf{i}", [128, KT * 130], BF16) for i in range(2)]
    pts = [ar.alloc(f"pts{i}", [128, 512], BF16) for i in range(4)]
    o_fm = ar.alloc("o_fm", [128, 8 * OWN], BF16)
    drow = ar.alloc("drow", [1, 512], F32)
    rbc = ar.alloc("rbc", [128, 512], F32)
    onesf = ar.alloc("onesf", [1, 64], F32)
    t_onesf = OP("vector", lambda e: e.memset(onesf[0:1, :], 1.0), tick="dve")
    pbr = pbt[0].bitcast(F32)

    qfv = q_fm.ap().rearrange("p (c t) -> p c t", c=8, t=OWN)
    ofv = o_fm.ap().rearrange("p (c t) -> p c t", c=8, t=OWN)

    r_kv = Ring(2)
    r_eslab = Ring(2)
    r_pts = Ring(2)
    r_obank = Ring(2)
    r_drow = Ring(1)
    r_rbc = Ring(1)
    r_pbr = Ring(1)

    pair_tick = {}
    a_dma_done = S_.count("d_sp")

    def load_pair(p):
        t1 = OP("sync", lambda e, p=p: e.dma_start(
            kbuf[p % 2][:], k_scr[p * 128:(p + 1) * 128, :]),
            waits=[r_kv.wait(p), a_dma_done], tick="d_sp")
        t2 = OP("gpsimd", lambda e, p=p: e.dma_start(
            vbuf[p % 2][:], v_scr[p, :, :]),
            waits=[r_kv.wait(p), a_dma_done], tick="d_gp")
        pair_tick[p] = [t1, t2]

    load_pair(0)
    load_pair(1)
    nslab = KT
    obank_free = {0: None, 1: None}
    eix = 0
    for p in range(8):
        for hh in range(2):
            bs = hh * 64
            for qp in range(2):
                exp_t = {0: {}, 1: {}}
                pv_t = {0: {}, 1: {}}
                pend = []
                for sl in range(nslab):
                    t_qk = {}
                    for st in range(2):
                        qh = 2 * qp + st
                        eb = pb[2 * st + sl % 2]
                        if sl > 1:
                            w = [exp_t[st][sl - 2]]
                        else:
                            w = list(pair_tick[p]) + [obank_free[st]]
                        OP("tensor", lambda e, kt=sl, eb=eb, bs=bs,
                           p=p, qh=qh: e.matmul(
                            eb[:, 0:512],
                            kbuf[p % 2][bs:bs + 64, kt * 128:(kt + 1) * 128],
                            qfv[bs:bs + 64, p, qh * 512:qh * 512 + 512],
                            start=True, stop=True),
                           waits=w, tick="pe")
                        t_qk[st] = S_.count("pe")
                    for fn in pend:
                        fn()
                    pend = []
                    for st in range(2):
                        spt = pts[2 * st + sl % 2]
                        eb = pb[2 * st + sl % 2]
                        t_exp = OP("scalar", lambda e, spt=spt, eb=eb:
                                   e.activation(
                            spt[:, 0:512], eb[:, 0:512], AF.Exp,
                            scale=1.0 / 32.0),
                            waits=[t_qk[st], pv_t[st].get(sl - 2)], tick="act")
                        exp_t[st][sl] = t_exp

                        def mk_pv(sl=sl, spt=spt, t_exp=t_exp, st=st, hh=hh, p=p):
                            def em():
                                OP("tensor", lambda e, kt=sl, spt=spt, hh=hh,
                                   p=p, st=st: e.matmul(
                                    pb[4 + st][0:65, 0:512],
                                    vbuf[p % 2][:, kt * 130 + hh * 65:
                                                kt * 130 + hh * 65 + 65],
                                    spt[:, 0:512],
                                    start=(sl == 0),
                                    stop=(sl == nslab - 1),
                                    skip_group_check=True),
                                   waits=[t_exp], tick="pe")
                                t = S_.count("pe")
                                pv_t[st][sl] = t
                            return em
                        pend.append(mk_pv())
                for fn in pend:
                    fn()
                for st in range(2):
                    qh = 2 * qp + st
                    ob = 4 + st
                    t_d1 = OP("vector", lambda e, ob=ob: e.tensor_scalar(
                        drow[0:1, 0:512], pb[ob][64:65, 0:512], 1e-30,
                        None, ALU.max, ALU.bypass),
                        waits=[pv_t[st][nslab - 1], r_drow.wait(eix)], tick="dve")
                    t_d2 = OP("vector", lambda e: e.reciprocal(
                        drow[0:1, 0:512], drow[0:1, 0:512]),
                        waits=[t_d1], tick="dve")
                    OP("tensor", lambda e: e.matmul(
                        pbr[0:64, 0:512], onesf[0:1, 0:64],
                        drow[0:1, 0:512],
                        start=True, stop=True),
                       waits=[t_d2, t_onesf, r_pbr.wait(eix)], tick="pe")
                    t_bc = S_.count("pe")
                    r_drow.done(eix, t_bc)
                    t_rc = OP("vector", lambda e: e.tensor_copy(
                        rbc[0:64, 0:512], pbr[0:64, 0:512]),
                        waits=[t_bc, r_rbc.wait(eix)], tick="dve")
                    r_pbr.done(eix, t_rc)
                    t_on = OP("vector", lambda e, ob=ob, bs=bs, p=p, qh=qh:
                              e.tensor_tensor(
                        ofv[bs:bs + 64, p, qh * 512:qh * 512 + 512],
                        pb[ob][0:64, 0:512], rbc[0:64, 0:512], ALU.mult),
                        waits=[t_rc], tick="dve")
                    obank_free[st] = t_on
                    r_rbc.done(eix, t_on)
                    eix += 1
        if p + 2 < 8:
            r_kv.done(p, S_.count("pe"))
            load_pair(p + 2)

    S_.barrier(ENG)
    ar.reset(mB)

    mB2 = ar.mark()
    s_boutbc = ar.alloc("s_boutbc", [128, D], F32)
    s_bout_bf = ar.alloc("s_bout_bf", [1, D], BF16)
    c_bo = load(s_bout_bf[:], dr["b_out"][:], q="gpsimd")
    tb = bcast_row_to_tile(s_bout_bf[0:1, 0:512], s_boutbc[:, 0:512], 512, [c_bo], 4)
    boutbc_done = bcast_row_to_tile(s_bout_bf[0:1, 512:1024], s_boutbc[:, 512:1024],
                                    512, [c_bo], 4)
    wout_b = ar.alloc("wout_b", [128, 8 * D], BF16)
    xstg = [ar.alloc(f"xstgB{i}", [128, D], F32) for i in range(2)]
    xnstg = [ar.alloc(f"xnstgB{i}", [128, D], F32) for i in range(2)]
    opstg = [ar.alloc(f"opstgB{i}", [128, D], F32) for i in range(2)]
    for ci in range(8):
        load(wout_b[:, ci * D:(ci + 1) * D], dr["Wout_t"][ci * 128:(ci + 1) * 128, :],
             q="gpsimd")
    t_wo = S_.count("d_gp")
    r_b2bank = Ring(4)
    r_b2stg = Ring(2)
    r_bx = Ring(2)
    r_bxn = Ring(2)
    for ti in range(16):
        sl = ti % 2
        t_x = OP("sync", lambda e, ti=ti, sl=sl: e.dma_start(
            xstg[sl][:], dr["xcore"][QROW0 + ti * 128:QROW0 + (ti + 1) * 128, :]),
            waits=[r_bx.wait(ti)], tick="d_sp")
        t_xn = OP("gpsimd", lambda e, ti=ti, sl=sl: e.dma_start(
            xnstg[sl][:], xn_scr[1280 + ti * 128:1280 + (ti + 1) * 128, :]),
            waits=[r_bxn.wait(ti), a_dma_done], tick="d_gp")
        for half in range(2):
            bank = sl * 2 + half
            for ci in range(8):
                OP("tensor", lambda e, ci=ci, half=half, ti=ti, bank=bank: e.matmul(
                    pb[bank][:, 0:512],
                    ofv[:, ci, ti * 128:(ti + 1) * 128],
                    wout_b[:, ci * D + half * 512:ci * D + half * 512 + 512],
                    start=(ci == 0), stop=(ci == 7)),
                   waits=([t_wo, r_b2bank.wait(bank)] if ci == 0 else []),
                   tick=("pe" if ci == 7 else None))
        t_mm = S_.count("pe")
        for half in range(2):
            bank = sl * 2 + half
            hs = slice(half * 512, half * 512 + 512)
            OP("vector", lambda e, bank=bank, hs=hs, sl=sl: e.tensor_tensor(
                opstg[sl][:, hs], pb[bank][:, 0:512], s_boutbc[:, hs], ALU.add),
               waits=[t_mm, boutbc_done, r_b2stg.wait(ti)], tick="dve")
            OP("vector", lambda e, hs=hs, sl=sl: e.tensor_tensor(
                opstg[sl][:, hs], opstg[sl][:, hs], xnstg[sl][:, hs], ALU.add),
               waits=[t_xn], tick="dve")
            t_ep = OP("vector", lambda e, hs=hs, sl=sl: e.tensor_tensor(
                opstg[sl][:, hs], opstg[sl][:, hs], xstg[sl][:, hs], ALU.add),
               waits=[t_x], tick="dve")
            r_b2bank.done(bank, t_ep)
        r_bx.done(ti, t_ep)
        r_bxn.done(ti, t_ep)
        t_d = OP("sync", lambda e, ti=ti, sl=sl: e.dma_start(
            outp_scr[ti * 128:(ti + 1) * 128, :], opstg[sl][:]),
            waits=[t_ep], tick="d_sp")
        r_b2stg.done(ti, t_d)

    outp_done = [S_.count("d_sp")]
    S_.barrier(ENG)
    ar.reset(mB2)
    ar.reset_top(topQ)

    AST = T1 * 128 + 8
    xinfm_l1 = None

    def ffn_layer(l, Ti, To, qoff, xin_st, in_scr, in_off, out_kind, scr_ready):
        nonlocal xinfm_l1
        xin_fm = xnff if l == 0 else xinfm_l1
        m0 = ar.mark()
        xiv = xin_fm.ap().rearrange("p (c t) -> p c t", c=8, t=xin_st)

        qsrow = ar.alloc("qsrow", [1, 4 * CH], BF16)
        t_qs = load(qsrow[0:1, 0:CH], dr[f"qs{l}"][:], q="gpsimd")
        t_ks = load(qsrow[0:1, 2 * CH:3 * CH], dr[f"ks{l}"][:], q="gpsimd")
        OP("vector", lambda e: e.tensor_copy(
            qsrow.ap()[0:1, CH:2 * CH].rearrange("p (a b) -> p a b", a=2),
            qsrow.ap()[0:1, 0:CH].rearrange("p (a b) -> p a b", a=2)[:, ::-1, :]),
           waits=[t_qs], tick="dve")
        t_sr = OP("vector", lambda e: e.tensor_copy(
            qsrow.ap()[0:1, 3 * CH:4 * CH].rearrange("p (a b) -> p a b", a=2),
            qsrow.ap()[0:1, 2 * CH:3 * CH].rearrange("p (a b) -> p a b", a=2)[:, ::-1, :]),
           waits=[t_ks], tick="dve")
        sc_bc = ar.alloc("sc_bc", [128, 4 * CH], BF16)
        t_scbc = None
        for i in range(4):
            t_scbc = bcast_row_to_tile(qsrow[0:1, i * CH:(i + 1) * CH],
                                       sc_bc[:, i * CH:(i + 1) * CH], CH, [t_sr], 4)

        ofm = ar.alloc("ofm", [128, 4 * To * 128], BF16)
        arow = ar.alloc("arow", [1, To * 128], BF16)
        brow = ar.alloc("brow", [1, To * 128], BF16)
        mKV = ar.mark()
        kfm = ar.alloc("kfm", [128, 4 * Ti * 128], BF16)
        qfm2 = ar.alloc("qfm2", [128, 4 * Ti * 128], BF16)
        vaug = ar.alloc("vaug", [128, Ti * 520], BF16)
        mC1 = ar.mark()
        wqkv_b = ar.alloc("wqkv_b", [128, 8 * 1536], BF16)
        for ci in range(8):
            load(wqkv_b[:, ci * 1536:(ci + 1) * 1536],
                 dr[f"wqkv{l}"][ci * 128:(ci + 1) * 128, :], q="gpsimd")
        t_wq = S_.count("d_gp")
        tblf = [ar.alloc("tblf0", [128, 4 * CH], F32)] * 2
        tblb = ar.alloc("tblb", [128, 4 * CH], BF16)
        qk_tm = ar.alloc("qk_tm", [128, 1024], BF16)
        sq_tm = ar.alloc("sq_tm", [128, 1024], BF16)
        u_tm = ar.alloc("u_tm", [128, 512], BF16)
        v1_tm = ar.alloc("v1_tm", [128, 512], BF16)
        rst = [ar.alloc(f"rstC{i}", [128, 48], F32) for i in range(2)]
        kvv = kfm.ap().rearrange("p (c t) -> p c t", c=4)
        qvv = qfm2.ap().rearrange("p (c t) -> p c t", c=4)
        vav = vaug.ap().rearrange("p (t h c) -> p t h c", t=Ti, h=8)

        r_qkvbank = Ring(2)
        r_tbl = Ring(1)
        r_qktm = Ring(1)
        r_rst = Ring(2)
        r_pbtC = Ring(2)

        for ti in range(Ti):
            b0 = (ti % 2) * 3
            for ci in range(8):
                coff = ti * 128 + (1 if ci < 4 else 0)
                for nb in range(3):
                    OP("tensor", lambda e, ci=ci, nb=nb, b0=b0, coff=coff: e.matmul(
                        pb[b0 + nb][:, 0:512],
                        xiv[:, ci, coff:coff + 128],
                        wqkv_b[:, ci * 1536 + nb * 512:ci * 1536 + nb * 512 + 512],
                        start=(ci == 0), stop=(ci == 7)),
                       waits=([t_wq, r_qkvbank.wait(ti)] if (ci == 0 and nb == 0) else []),
                       tick=("pe" if (ci == 7 and nb == 2) else None))
            t_qkv = S_.count("pe")
            OP("vector", lambda e, b0=b0: e.tensor_copy(
                qk_tm[:, 0:512], pb[b0][:, 0:512]),
               waits=[t_qkv, r_qktm.wait(ti - 1)], tick="dve")
            t_qke = OP("vector", lambda e, b0=b0: e.tensor_copy(
                qk_tm[:, 512:1024], pb[b0 + 1][:, 0:512]), tick="dve")
            OP("vector", lambda e, b0=b0, ti=ti: e.tensor_copy(
                vav[:, ti, :, 0:64],
                pb[b0 + 2][:, 0:512].rearrange("p (h c) -> p h c", h=8)),
               tick="dve")
            vcol = ti + (T2 - Ti)
            t_ve = OP("vector", lambda e, ti=ti, vcol=vcol: e.tensor_copy(
                vav[:, ti, :, 64:65],
                s_vld.ap()[:, vcol:vcol + 1].unsqueeze(1).broadcast_to((128, 8, 1))),
               waits=[c_vld], tick="dve")
            r_qkvbank.done(ti, t_ve)
            rr = rst[ti % 2]
            t_sq1 = OP("scalar", lambda e: e.activation(
                sq_tm[:, 0:512], qk_tm[:, 0:512], AF.Square),
                waits=[t_qke, r_rst.wait(ti)], tick="act")
            t_sq2 = OP("scalar", lambda e: e.activation(
                sq_tm[:, 512:1024], qk_tm[:, 512:1024], AF.Square), tick="act")
            OP("vector", lambda e, rr=rr: e.tensor_reduce(
                rr[:, 0:8], sq_tm.ap()[:, 0:512].rearrange("p (h c) -> p h c", h=8),
                AX.X, ALU.add), waits=[t_sq1, t_sq2], tick="dve")
            t_red = OP("vector", lambda e, rr=rr: e.tensor_reduce(
                rr[:, 8:16], sq_tm.ap()[:, 512:1024].rearrange("p (h c) -> p h c", h=8),
                AX.X, ALU.add), tick="dve")
            t_sq3 = OP("scalar", lambda e, rr=rr: e.activation(
                rr[:, 16:32], rr[:, 0:16], AF.Sqrt), waits=[t_red], tick="act")
            OP("vector", lambda e, rr=rr: e.tensor_scalar(
                rr[:, 16:32], rr[:, 16:32], 1e-12, None, ALU.max, ALU.bypass),
               waits=[t_sq3], tick="dve")
            t_rn = OP("vector", lambda e, rr=rr: e.reciprocal(
                rr[:, 32:48], rr[:, 16:32]), tick="dve")
            tsl = ti % 2
            t_tb = []
            for i, nm in enumerate(["cq", "sq", "ck", "sk"]):
                row0 = (ti + (T2 - Ti)) * 128
                t_tb.append(OP("sync", lambda e, i=i, nm=nm, row0=row0, tsl=tsl:
                               e.dma_start(tblf[tsl][:, i * CH:(i + 1) * CH],
                                           dr[nm][row0:row0 + 128, :]),
                               waits=[r_tbl.wait(ti)], tick="d_sp"))
            t_tbl = None
            for i in range(4):
                t_tbl = OP("vector", lambda e, i=i, tsl=tsl: e.tensor_tensor(
                    tblb[:, i * CH:(i + 1) * CH], tblf[tsl][:, i * CH:(i + 1) * CH],
                    sc_bc[:, i * CH:(i + 1) * CH], ALU.mult),
                    waits=[t_tb[i], t_scbc], tick="dve")
            r_tbl.done(ti, t_tbl)
            t_tr_last = None
            for side in range(2):
                for hh2 in range(8):
                    OP("vector", lambda e, side=side, hh2=hh2, rr=rr: e.tensor_scalar(
                        u_tm[:, hh2 * CH:(hh2 + 1) * CH],
                        qk_tm[:, side * 512 + hh2 * CH:side * 512 + hh2 * CH + CH],
                        rr[:, 32 + side * 8 + hh2:32 + side * 8 + hh2 + 1], None,
                        ALU.mult, ALU.bypass),
                       waits=([t_rn] if hh2 == 0 else []),
                       tick=("dve" if hh2 == 7 else None))
                OP("vector", lambda e, side=side: e.tensor_tensor(
                    v1_tm.ap()[:, 0:512].rearrange("p (h c) -> p h c", h=8),
                    u_tm.ap()[:, 0:512].rearrange("p (h c) -> p h c", h=8),
                    tblb.ap()[:, (side * 2) * CH:(side * 2) * CH + CH]
                        .unsqueeze(1).broadcast_to((128, 8, CH)),
                    ALU.mult), waits=[t_tbl], tick="dve")
                OP("vector", lambda e, side=side: e.tensor_tensor(
                    u_tm.ap()[:, 0:512].rearrange("p (h r c) -> p h r c", h=8, r=2),
                    u_tm.ap()[:, 0:512].rearrange("p (h r c) -> p h r c", h=8, r=2)[
                        :, :, ::-1, :],
                    tblb.ap()[:, (side * 2 + 1) * CH:(side * 2 + 1) * CH + CH]
                        .rearrange("p (r c) -> p r c", r=2)
                        .unsqueeze(1).broadcast_to((128, 8, 2, 32)),
                    ALU.mult), tick="dve")
                t_v2 = OP("vector", lambda e, side=side: e.tensor_tensor(
                    qk_tm[:, side * 512:side * 512 + 512], v1_tm[:, 0:512],
                    u_tm[:, 0:512], ALU.add), tick="dve")
                pt = pbt[(2 * ti + side) % 2]
                for bc in range(4):
                    OP("tensor", lambda e, side=side, bc=bc, pt=pt: e.transpose(
                        pt[:, bc * 128:(bc + 1) * 128],
                        qk_tm[:, side * 512 + bc * 128:side * 512 + bc * 128 + 128],
                        s_id_bf[:]),
                       waits=([t_v2, r_pbtC.wait(2 * ti + side)] if bc == 0 else []),
                       tick=("pe" if bc == 3 else None))
                t_tr = S_.count("pe")
                dst = (qvv if side == 0 else kvv)
                t_cp = OP("vector", lambda e, dst=dst, ti=ti, pt=pt: e.tensor_copy(
                    dst[:, :, ti * 128:(ti + 1) * 128],
                    pt[:, 0:512].rearrange("p (c t) -> p c t", c=4)),
                    waits=[t_tr], tick="dve")
                r_pbtC.done(2 * ti + side, t_cp)
                t_tr_last = t_cp
            r_qktm.done(ti, t_tr_last)
            r_qktm.done(ti, t_sq2)
            r_rst.done(ti, t_tr_last)
        if "TAP2" in DEV_SKIP and l == 0:
            tw = [S_.count("dve"), S_.count("pe"), S_.count("act")]
            OP("sync", lambda e: e.dma_start(tap_q[:, 0:4 * Ti * 128], qfm2[:]), waits=tw, tick="d_sp")
            OP("sync", lambda e: e.dma_start(tap_k[:, 0:4 * Ti * 128], kfm[:]), waits=tw, tick="d_sp")
            OP("sync", lambda e: e.dma_start(tap_v[:, 0:Ti * 520], vaug[:]), waits=tw, tick="d_sp")
        S_.barrier(ENG)
        ar.reset(mC1)
        if l == 0:
            ar.reset_top(topX)
        else:
            ar.reset_top(topY[0])

        ofv2 = ofm.ap().rearrange("p (c t) -> p c t", c=4)
        mcol = ar.alloc("mcol", [128, 2], F32)
        crow = ar.alloc("crow", [1, 4], BF16)
        biasc = ar.alloc("biasc", [128, 2], F32)
        ptl = [ar.alloc(f"ptl{i}", [128, 640], BF16) for i in range(2)]
        otm2 = [ar.alloc(f"otm2_{i}", [128, 64], BF16) for i in range(2)]
        ors2 = ar.alloc("ors2", [128, 4], F32)

        r_el = Ring(2)
        r_ol = Ring(2)
        r_ptl = Ring(2)
        r_mcol = Ring(2)
        r_crow = Ring(2)
        r_biasc = Ring(2)
        r_otm2 = Ring(2)
        r_pbtL = Ring(2)
        exp_tick = {}

        r_ors = Ring(1)
        its = [(h, h // 2, (h % 2) * 64, qt) for h in range(8) for qt in range(To)]
        N2 = len(its)
        st2 = [dict() for _ in range(N2)]

        def c2_s0(it):
            h, hp, bs, qt = its[it]
            sl = it % 2
            e0, e1 = pb[sl * 2], pb[sl * 2 + 1]
            g0 = qt + qoff - 4
            for gi in range(5):
                g = g0 + gi
                eb, ecol = (e0, gi * 128) if gi < 4 else (e1, 0)
                masked = gi in (0, 4)
                OP("tensor", lambda e, g=g, eb=eb, ecol=ecol, hp=hp, bs=bs, qt=qt,
                   masked=masked: e.matmul(
                    eb[:, ecol:ecol + 128],
                    kvv[bs:bs + 64, hp, g * 128:(g + 1) * 128],
                    qvv[bs:bs + 64, hp, (qt + qoff) * 128:(qt + qoff) * 128 + 128],
                    start=True, stop=not masked),
                   waits=([r_el.wait(it)] if gi == 0 else []), tick=None)
                if masked:
                    OP("tensor", lambda e, eb=eb, gi=gi: e.matmul(
                        eb[:, 0:128], s_id_bf[:],
                        s_tri[:, 0:128] if gi == 0 else s_tri[:, 128:256],
                        start=False, stop=True),
                       waits=[c_tri, c_id], tick=("pe" if gi == 4 else None))
            st2[it]["qk"] = S_.count("pe")

        def c2_sa(it):
            h, hp, bs, qt = its[it]
            sl = it % 2
            e0, e1 = pb[sl * 2], pb[sl * 2 + 1]
            OP("vector", lambda e, e0=e0, sl=sl: e.tensor_reduce(
                mcol[:, sl:sl + 1], e0[:, 0:512], AX.X, ALU.max),
               waits=[st2[it]["qk"], r_mcol.wait(it)], tick="dve")
            OP("vector", lambda e, e1=e1, sl=sl: e.tensor_reduce(
                ors2[:, 2 + sl:3 + sl], e1[:, 0:128], AX.X, ALU.max), tick="dve")
            t_mc = OP("vector", lambda e, sl=sl: e.tensor_tensor(
                mcol[:, sl:sl + 1], mcol[:, sl:sl + 1], ors2[:, 2 + sl:3 + sl],
                ALU.max), tick="dve")
            eb_m = pb[(1 - sl) * 2 + 1]
            OP("tensor", lambda e, eb_m=eb_m, sl=sl: e.matmul(
                eb_m[0:1, 256:384], mcol[:, sl:sl + 1], s_idm8[:],
                start=True, stop=True),
               waits=[t_mc, c_im8, exp_tick.get(it - 1), r_crow.wait(it)],
               tick="pe")
            t_mt = S_.count("pe")
            t_cred = OP("vector", lambda e, eb_m=eb_m, sl=sl: e.tensor_reduce(
                crow[0:1, sl:sl + 1], eb_m[0:1, 256:384], AX.X, ALU.min),
                waits=[t_mt], tick="dve")
            r_mcol.done(it, t_cred)
            OP("tensor", lambda e, eb_m=eb_m, sl=sl: e.matmul(
                eb_m[:, 384:385], s_ones[0:1, :], crow[0:1, sl:sl + 1],
                start=True, stop=True), waits=[t_cred], tick="pe")
            t_bb = S_.count("pe")
            r_crow.done(it, t_bb)
            t_bc = OP("scalar", lambda e, eb_m=eb_m, sl=sl: e.activation(
                biasc[:, sl:sl + 1], eb_m[:, 384:385], AF.Copy, scale=1.0),
                waits=[t_bb, r_biasc.wait(it)], tick="act")
            OP("scalar", lambda e, e0=e0, sl=sl: e.activation(
                ptl[sl][:, 0:512], e0[:, 0:512], AF.Exp, scale=8.0,
                bias=biasc[:, sl:sl + 1]),
               waits=[t_bc, r_ptl.wait(it)], tick="act")
            t_exp = OP("scalar", lambda e, e1=e1, sl=sl: e.activation(
                ptl[sl][:, 512:640], e1[:, 0:128], AF.Exp, scale=8.0,
                bias=biasc[:, sl:sl + 1]), tick="act")
            exp_tick[it] = t_exp
            st2[it]["exp"] = t_exp
            r_el.done(it, t_exp)
            r_biasc.done(it, t_exp)

        def c2_sb(it):
            h, hp, bs, qt = its[it]
            sl = it % 2
            ob = 4 + sl
            g0 = qt + qoff - 4
            t_exp = st2[it]["exp"]
            for gi in range(5):
                g = g0 + gi
                OP("tensor", lambda e, g=g, gi=gi, sl=sl, h=h, ob=ob: e.matmul(
                    pb[ob][:, 0:65],
                    ptl[sl][:, gi * 128:gi * 128 + 128],
                    vaug[:, g * 520 + h * 65:g * 520 + h * 65 + 65],
                    start=(gi == 0), stop=(gi == 4), skip_group_check=True),
                   waits=([t_exp, r_ol.wait(it)] if gi == 0 else []),
                   tick=("pe" if gi == 4 else None))
            t_pv = S_.count("pe")
            r_ptl.done(it, t_pv)
            OP("vector", lambda e, ob=ob: e.tensor_scalar(
                ors2[:, 0:1], pb[ob][:, 64:65], 1e-30, None, ALU.max, ALU.bypass),
               waits=[t_pv, r_ors.wait(it)], tick="dve")
            t_rc = OP("vector", lambda e: e.reciprocal(ors2[:, 1:2], ors2[:, 0:1]),
                      tick="dve")
            t_on = OP("scalar", lambda e, ob=ob, sl=sl: e.activation(
                otm2[sl][:], pb[ob][:, 0:64], AF.Copy, scale=ors2[:, 1:2]),
                waits=[t_rc, r_otm2.wait(it)], tick="act")
            r_ol.done(it, t_on)
            r_ors.done(it, t_on)
            OP("tensor", lambda e, bs=bs, sl=sl: e.transpose(
                pbt[sl][bs:bs + 64, 0:128], otm2[sl][:], s_id_bf[:]),
               waits=[t_on, r_pbtL.wait(it)], tick="pe")
            t_otr = S_.count("pe")
            r_otm2.done(it, t_otr)
            t_oc = OP("scalar", lambda e, bs=bs, hp=hp, qt=qt, sl=sl: e.activation(
                ofv2[bs:bs + 64, hp, qt * 128:(qt + 1) * 128],
                pbt[sl][bs:bs + 64, 0:128], AF.Copy, scale=1.0),
                waits=[t_otr], tick="act")
            r_pbtL.done(it, t_oc)

        for slot in range(N2 + 2):
            if slot < N2:
                c2_s0(slot)
            if 1 <= slot <= N2:
                c2_sa(slot - 1)
            if slot >= 2:
                c2_sb(slot - 2)
        if "TAP2" in DEV_SKIP and l == 0:
            OP("sync", lambda e: e.dma_start(tap_o[:, 0:4 * To * 128], ofm[:]),
               waits=[S_.count("dve"), S_.count("pe"), S_.count("act")], tick="d_sp")
        S_.barrier(ENG)
        ar.reset(mKV)

        afm = ar.alloc("afm", [128, 8 * AST], BF16)
        mC3 = ar.mark()
        wo_b = ar.alloc("wo_b", [128, 4 * D], BF16)
        for ci in range(4):
            load(wo_b[:, ci * D:(ci + 1) * D], dr[f"wo{l}"][ci * 128:(ci + 1) * 128, :],
                 q="gpsimd")
        t_wob = S_.count("d_gp")
        avv = afm.ap().rearrange("p (c t) -> p c t", c=8, t=AST)
        xin_stg = [ar.alloc(f"xinstg{i}", [128, D], F32) for i in range(2)]
        a_stg = [ar.alloc(f"astg{i}", [128, D], F32) for i in range(2)]
        ab_stg = [ar.alloc(f"abstg{i}", [128, D], BF16) for i in range(2)]
        stt4 = [ar.alloc(f"stt4_{i}", [128, 4], F32) for i in range(2)]
        srow = [ar.alloc(f"srow{i}", [1, To * 128 + 8], F32) for i in range(4)]
        OP("vector", lambda e: e.memset(avv[:, :, 0:1], 0.0), tick="dve")
        for i in range(4):
            OP("vector", lambda e, i=i: e.memset(srow[i][0:1, :], 0.0), tick="dve")
        t_sr0 = S_.count("dve")

        if "TAP2" in DEV_SKIP and l == 0:
            OP("sync", lambda e: e.dma_start(tap_wo[0], wo_b[:]),
               waits=[t_wob], tick="d_sp")
        r_c3bank = Ring(4)
        r_astg = Ring(2)
        r_xinstg = Ring(2)
        r_abstg = Ring(2)
        r_stt4 = Ring(2)
        r_pbtD = Ring(2)
        for ti in range(To):
            sl = ti % 2
            t_xi = OP("sync", lambda e, ti=ti, sl=sl: e.dma_start(
                xin_stg[sl][:],
                in_scr[(ti + in_off) * 128:(ti + in_off) * 128 + 128, :]),
                waits=[r_xinstg.wait(ti), scr_ready], tick="d_sp")
            for half in range(2):
                bank = sl * 2 + half
                for ci in range(4):
                    OP("tensor", lambda e, ci=ci, half=half, ti=ti, bank=bank: e.matmul(
                        pb[bank][:, 0:512], ofv2[:, ci, ti * 128:(ti + 1) * 128],
                        wo_b[:, ci * D + half * 512:ci * D + half * 512 + 512],
                        start=(ci == 0), stop=(ci == 3)),
                       waits=([t_wob, r_c3bank.wait(bank)] if ci == 0 else []),
                       tick=("pe" if ci == 3 else None))
            t_mm = S_.count("pe")
            t_a = None
            for half in range(2):
                bank = sl * 2 + half
                hs = slice(half * 512, half * 512 + 512)
                t_a = OP("vector", lambda e, bank=bank, hs=hs, sl=sl: e.tensor_tensor(
                    a_stg[sl][:, hs], pb[bank][:, 0:512], xin_stg[sl][:, hs], ALU.add),
                    waits=[t_mm, t_xi, r_astg.wait(ti)], tick="dve")
                r_c3bank.done(bank, t_a)
            r_xinstg.done(ti, t_a)
            t_adma = OP("sync", lambda e, ti=ti, sl=sl: e.dma_start(
                a_scr[ti * 128:(ti + 1) * 128, :], a_stg[sl][:]),
               waits=[t_a], tick="d_sp")
            r_astg.done(ti, t_adma)
            s4 = stt4[sl]
            OP("vector", lambda e, sl=sl, s4=s4: e.tensor_scalar(
                dmp[:, 0:512], a_stg[sl][:, 0:512], 1.0, None, ALU.mult, ALU.add,
                accum_out=s4[:, 0:1]), waits=[t_a, r_stt4.wait(ti)], tick="dve")
            OP("vector", lambda e, sl=sl, s4=s4: e.tensor_scalar(
                dmp[:, 0:512], a_stg[sl][:, 512:1024], 1.0, None, ALU.mult, ALU.add,
                accum_out=s4[:, 1:2]), tick="dve")
            OP("scalar", lambda e, sl=sl, s4=s4: e.activation(
                ab_stg[sl][:, 0:512], a_stg[sl][:, 0:512], AF.Square,
                accum_out=s4[:, 2:3]), waits=[t_a, r_abstg.wait(ti)], tick="act")
            t_s4 = OP("scalar", lambda e, sl=sl, s4=s4: e.activation(
                ab_stg[sl][:, 512:1024], a_stg[sl][:, 512:1024], AF.Square,
                accum_out=s4[:, 3:4]), tick="act")
            t_ab = OP("scalar", lambda e, sl=sl: e.copy(ab_stg[sl][:], a_stg[sl][:]),
                      waits=[t_a, r_abstg.wait(ti)], tick="act")
            r_astg.done(ti, t_ab)
            pt = pbt[sl]
            for c in range(8):
                OP("tensor", lambda e, c=c, sl=sl, pt=pt: e.transpose(
                    pt[:, c * 128:(c + 1) * 128], ab_stg[sl][:, c * 128:(c + 1) * 128],
                    s_id_bf[:]),
                   waits=([t_ab, r_pbtD.wait(ti)] if c == 0 else []),
                   tick=("pe" if c == 7 else None))
            t_tra = S_.count("pe")
            r_abstg.done(ti, t_tra)
            t_cpa = OP("vector", lambda e, ti=ti, pt=pt: e.tensor_copy(
                avv[:, :, 1 + ti * 128:1 + (ti + 1) * 128],
                pt[:, 0:1024].rearrange("p (c t) -> p c t", c=8)),
               waits=[t_tra], tick="dve")
            r_pbtD.done(ti, t_cpa)
            for k in range(4):
                OP("tensor", lambda e, k=k, s4=s4: e.transpose(
                    pb[4 + (0 if True else 0)][0:1, 0:128] if False else
                    pb[4][0:1, 0:128],
                    s4[:, k:k + 1], s_id_f[:]),
                   waits=[t_s4, c_idf, r_c3bank.wait(4) if False else None,
                          S_.count("dve")],
                   tick="pe")
                t_trs = S_.count("pe")
                OP("vector", lambda e, k=k, ti=ti: e.tensor_copy(
                    srow[k][0:1, 8 + ti * 128:8 + (ti + 1) * 128], pb[4][0:1, 0:128]),
                   waits=[t_trs, t_sr0], tick="dve")
            t_done4 = S_.count("dve")
            r_stt4.done(ti, t_done4)
        S_.barrier(ENG)

        murow = ar.alloc("murow", [1, To * 128], F32)
        varow = ar.alloc("varow", [1, To * 128], F32)
        R = To * 128
        OP("vector", lambda e: e.tensor_tensor(
            murow[0:1, :], srow[0][0:1, 8:8 + R], srow[1][0:1, 7:7 + R], ALU.add),
           tick="dve")
        OP("vector", lambda e: e.tensor_scalar(
            murow[0:1, :], murow[0:1, :], 1.0 / D, None, ALU.mult, ALU.bypass),
           tick="dve")
        OP("vector", lambda e: e.tensor_tensor(
            varow[0:1, :], srow[2][0:1, 8:8 + R], srow[3][0:1, 7:7 + R], ALU.add),
           tick="dve")
        OP("vector", lambda e: e.tensor_scalar(
            varow[0:1, :], varow[0:1, :], 1.0 / D, None, ALU.mult, ALU.bypass),
           tick="dve")
        OP("vector", lambda e: e.tensor_tensor(
            srow[0][0:1, 8:8 + R], murow[0:1, :], murow[0:1, :], ALU.mult),
           tick="dve")
        OP("vector", lambda e: e.tensor_tensor(
            varow[0:1, :], varow[0:1, :], srow[0][0:1, 8:8 + R], ALU.subtract),
           tick="dve")
        t_vv = OP("vector", lambda e: e.tensor_scalar(
            varow[0:1, :], varow[0:1, :], 0.0, 1e-5, ALU.max, ALU.add), tick="dve")
        t_sv = OP("scalar", lambda e: e.activation(
            varow[0:1, :], varow[0:1, :], AF.Sqrt), waits=[t_vv], tick="act")
        OP("vector", lambda e: e.reciprocal(varow[0:1, :], varow[0:1, :]),
           waits=[t_sv], tick="dve")
        OP("vector", lambda e: e.tensor_copy(arow[0:1, :], varow[0:1, :]), tick="dve")
        OP("vector", lambda e: e.tensor_tensor(
            murow[0:1, :], murow[0:1, :], varow[0:1, :], ALU.mult), tick="dve")
        t_ab2 = OP("vector", lambda e: e.tensor_scalar(
            brow[0:1, :], murow[0:1, :], -1.0, None, ALU.mult, ALU.bypass), tick="dve")

        if "TAP2" in DEV_SKIP and l == 0:
            OP("sync", lambda e: e.dma_start(tap_wo[1], wo_b[:]),
               waits=[S_.count("dve"), S_.count("pe"), S_.count("act")], tick="d_sp")
        S_.barrier(ENG)
        ar.reset(mC3)

        if out_kind == "x1" and xinfm_l1 is None:
            topY.append(ar.mark_top())
            xinfm_l1 = ar.alloc_top("xinfm_l1", [128, 8 * AST], BF16)
            OP("vector", lambda e: e.memset(
                xinfm_l1.ap().rearrange("p (c t) -> p c t", c=8, t=AST)[:, :, 0:1],
                0.0), tick="dve")
        mC4 = ar.mark()
        gam = ar.alloc("gam", [1, D], BF16)
        t_g = load(gam[0:1, :], dr[f"lng_row{l}"][:], q="sync")
        lnb = ar.alloc("lnb", [128, 8], F32)
        t_lb = load(lnb[:], dr[f"lnb_col{l}"][:], q="sync")
        b1c = ar.alloc("b1c", [128, F1 // 128], F32)
        t_b1 = load(b1c[:], dr[f"b1p{l}"][:], q="sync")
        w1buf = ar.alloc("w1buf", [128, 8 * 6 * 256], BF16)
        w2buf = ar.alloc("w2buf", [128, 6 * D], BF16)
        hgate = [ar.alloc(f"hgate{i}", [128, 6 * 512], BF16) for i in range(2)]
        ggel = [ar.alloc(f"ggel{i}", [128, 512], BF16) for i in range(2)]
        x2stg = [ar.alloc(f"x2stg{i}", [128, D], F32) for i in range(2)]
        ngroups = (To * 128 + 511) // 512
        wq_guard = [None]

        r_hbk = Ring(4)
        r_x2bk = Ring(2)
        r_ggel = Ring(2)
        r_hg = Ring(2)
        r_x2stg = Ring(2)
        r_gbk = Ring(2)

        for grp in range(ngroups):
            t0 = grp * 512
            gn = min(512, To * 128 - t0)
            for ci in range(8):
                OP("tensor", lambda e, ci=ci, t0=t0, gn=gn: e.matmul(
                    pb[4][:, 0:gn], gam[0:1, ci * 128:(ci + 1) * 128],
                    arow[0:1, t0:t0 + gn], start=True, stop=True),
                   waits=[t_ab2, t_g, r_gbk.wait(0)], tick=None)
                OP("tensor", lambda e, ci=ci, t0=t0, gn=gn: e.matmul(
                    pb[5][:, 0:gn], gam[0:1, ci * 128:(ci + 1) * 128],
                    brow[0:1, t0:t0 + gn], start=True, stop=True),
                   waits=[r_gbk.wait(1)], tick="pe")
                t_gb = S_.count("pe")
                co = 1 + t0 - (0 if ci < 4 else 1)
                OP("vector", lambda e, ci=ci, co=co, gn=gn: e.tensor_tensor(
                    dmp[:, 0:gn], avv[:, ci, co:co + gn], pb[4][:, 0:gn], ALU.mult),
                   waits=[t_gb], tick="dve")
                t_hh = OP("vector", lambda e, ci=ci, co=co, gn=gn:
                          e.scalar_tensor_tensor(
                    avv[:, ci, co:co + gn], dmp[:, 0:gn], lnb[:, ci:ci + 1],
                    pb[5][:, 0:gn], ALU.add, ALU.add), waits=[t_lb], tick="dve")
                r_gbk.done(0, t_hh)
                r_gbk.done(1, t_hh)
        t_hall = S_.count("dve")

        for halfp, (fp0, fpn) in enumerate([(0, 6), (6, 5), (11, 6), (17, 5)]):
            w_free = wq_guard[0]
            for ci in range(8):
                OP("gpsimd", lambda e, ci=ci, fp0=fp0, fpn=fpn: e.dma_start(
                    w1buf[:, ci * 6 * 256:ci * 6 * 256 + fpn * 256],
                    dr[f"w1p{l}"][ci * 128:(ci + 1) * 128,
                                  fp0 * 256:(fp0 + fpn) * 256]),
                   waits=[w_free], tick="d_gp")
            t_w1 = S_.count("d_gp")
            for hc in range(fpn):
                OP("gpsimd", lambda e, hc=hc, fp0=fp0: e.dma_start(
                    w2buf[:, hc * D:(hc + 1) * D],
                    dr[f"w2p{l}"][(fp0 + hc) * 128:(fp0 + hc + 1) * 128, :]),
                   waits=[w_free], tick="d_gp")
            t_w2 = S_.count("d_gp")
            for grp in range(ngroups):
                t0 = grp * 512
                gn = min(512, To * 128 - t0)
                gtiles = gn // 128
                g2 = halfp * ngroups + grp
                hg = hgate[g2 % 2]
                for fp in range(fpn):
                    fpg = fp0 + fp
                    slb = fp % 2
                    gx = g2 * 6 + fp
                    for hag in range(2):
                        bank = slb * 2 + hag
                        fcol = 2 * fp * 128 + hag * 128
                        for ci in range(8):
                            co = 1 + t0 - (0 if ci < 4 else 1)
                            OP("tensor", lambda e, ci=ci, fcol=fcol, bank=bank, gn=gn,
                               co=co: e.matmul(
                                pb[bank][:, 0:gn],
                                w1buf[:, ci * 6 * 256 + fcol:
                                      ci * 6 * 256 + fcol + 128],
                                avv[:, ci, co:co + gn], start=(ci == 0), stop=(ci == 7)),
                               waits=([t_w1, t_hall, r_hbk.wait(bank)] if ci == 0 else []),
                               tick=("pe" if ci == 7 else None))
                    t_h1 = S_.count("pe")
                    OP("scalar", lambda e, slb=slb, fpg=fpg, gn=gn, gx=gx: e.activation(
                        ggel[gx % 2][:, 0:gn], pb[slb * 2 + 1][:, 0:gn], AF.Gelu,
                        bias=b1c[:, 2 * fpg + 1:2 * fpg + 2]),
                       waits=[t_h1, t_b1, r_ggel.wait(gx)], tick="act")
                    t_gl = S_.count("act")
                    t_hgt = OP("vector", lambda e, slb=slb, fp=fp, fpg=fpg, gn=gn,
                               gx=gx, hg=hg: e.scalar_tensor_tensor(
                                hg[:, fp * 512:fp * 512 + gn], pb[slb * 2][:, 0:gn],
                                b1c[:, 2 * fpg:2 * fpg + 1], ggel[gx % 2][:, 0:gn],
                                ALU.add, ALU.mult),
                               waits=[t_gl, r_hg.wait(g2)], tick="dve")
                    r_ggel.done(gx, t_hgt)
                    r_hbk.done(slb * 2, t_hgt)
                    r_hbk.done(slb * 2 + 1, t_hgt)
                t_hgall = S_.count("dve")
                for tt in range(gtiles):
                    ti = grp * 4 + tt
                    for halfd in range(2):
                        bank = 4 + halfd
                        for hc in range(fpn):
                            OP("tensor", lambda e, hc=hc, tt=tt, halfd=halfd, bank=bank,
                               fpn=fpn, hg=hg: e.matmul(
                                pb[bank][:, 0:512],
                                hg[:, hc * 512 + tt * 128:hc * 512 + tt * 128 + 128],
                                w2buf[:, hc * D + halfd * 512:
                                      hc * D + halfd * 512 + 512],
                                start=(hc == 0), stop=(hc == fpn - 1)),
                               waits=([t_hgall, t_w2, r_x2bk.wait(halfd)]
                                      if hc == 0 else []),
                               tick=("pe" if hc == fpn - 1 else None))
                    t_x2 = S_.count("pe")
                    sl = tt % 2
                    OP("vector", lambda e, sl=sl: e.tensor_copy(
                        x2stg[sl][:, 0:512], pb[4][:, 0:512]),
                       waits=[t_x2, r_x2stg.wait(tt)], tick="dve")
                    t_xc = OP("vector", lambda e, sl=sl: e.tensor_copy(
                        x2stg[sl][:, 512:1024], pb[5][:, 0:512]), tick="dve")
                    r_x2bk.done(0, t_xc)
                    r_x2bk.done(1, t_xc)
                    if halfp == 0:
                        t_dx = OP("gpsimd", lambda e, ti=ti, sl=sl: e.dma_start(
                            x2_scr[ti * 128:(ti + 1) * 128, :], x2stg[sl][:]),
                            waits=[t_xc], tick="d_gp")
                    else:
                        t_dx = OP("gpsimd", lambda e, ti=ti, sl=sl: e.dma_start(
                            x2_scr[ti * 128:(ti + 1) * 128, :], x2stg[sl][:],
                            accum_op=ALU.add), waits=[t_xc], tick="d_gp")
                    r_x2stg.done(tt, t_dx)
                r_hg.done(g2, S_.count("pe"))
            wq_guard[0] = S_.count("pe")
        t_x2_all = S_.count("d_gp")
        S_.barrier(ENG)
        ar.reset(mC4)

        b2bc = ar.alloc("b2bc", [128, D], F32)
        b2row = ar.alloc("b2row", [1, D], BF16)
        t_b2r = load(b2row[0:1, :], dr[f"b2_{l}"][:], q="gpsimd")
        t_b2bc = None
        for halfd in range(2):
            t_b2bc = bcast_row_to_tile(b2row[0:1, halfd * 512:halfd * 512 + 512],
                                       b2bc[:, halfd * 512:halfd * 512 + 512], 512,
                                       [t_b2r], 4)
        xin2_stg = [ar.alloc(f"xinstg2_{i}", [128, D], F32) for i in range(2)]
        xo_stg = [ar.alloc(f"xostg{i}", [128, D], F32) for i in range(2)]
        x2l_stg = [ar.alloc(f"x2lstg{i}", [128, D], F32) for i in range(2)]
        xob_stg = [ar.alloc(f"xobstg{i}", [128, D], BF16) for i in range(2)]
        opstg2 = [ar.alloc(f"opstg2_{i}", [128, D], F32) for i in range(2)]
        xl1v = (xinfm_l1.ap().rearrange("p (c t) -> p c t", c=8, t=AST)
                if xinfm_l1 is not None else None)
        r_xo = Ring(2)
        r_xob = Ring(2)
        r_op2 = Ring(2)
        r_pbtE = Ring(2)
        r_xin2 = Ring(2)
        r_x2l = Ring(2)
        for ti in range(To):
            sl = ti % 2
            t_a2 = OP("sync", lambda e, ti=ti, sl=sl: e.dma_start(
                xin2_stg[sl][:], a_scr[ti * 128:(ti + 1) * 128, :]),
                waits=[r_xin2.wait(ti)], tick="d_sp")
            t_x2l = OP("sync", lambda e, ti=ti, sl=sl: e.dma_start(
                x2l_stg[sl][:], x2_scr[ti * 128:(ti + 1) * 128, :]),
                waits=[t_x2_all, r_x2l.wait(ti)], tick="d_sp")
            OP("vector", lambda e, sl=sl: e.tensor_tensor(
                xo_stg[sl][:], x2l_stg[sl][:], b2bc[:], ALU.add),
               waits=[t_x2l, t_b2bc, r_xo.wait(ti)], tick="dve")
            t_add = OP("vector", lambda e, sl=sl: e.tensor_tensor(
                xo_stg[sl][:], xo_stg[sl][:], xin2_stg[sl][:], ALU.add),
               waits=[t_a2], tick="dve")
            r_xin2.done(ti, t_add)
            r_x2l.done(ti, t_add)
            vcol = ti + (T2 - To)
            t_xo = OP("vector", lambda e, sl=sl, vcol=vcol: e.tensor_scalar(
                xo_stg[sl][:], xo_stg[sl][:], s_vld[:, vcol:vcol + 1], None,
                ALU.mult, ALU.bypass), tick="dve")
            if out_kind == "x1":
                t_d1 = OP("sync", lambda e, ti=ti, sl=sl: e.dma_start(
                    x1_scr[ti * 128:(ti + 1) * 128, :], xo_stg[sl][:]),
                    waits=[t_xo], tick="d_sp")
                t_xb = OP("scalar", lambda e, sl=sl: e.copy(
                    xob_stg[sl][:], xo_stg[sl][:]),
                    waits=[t_xo, r_xob.wait(ti)], tick="act")
                r_xo.done(ti, t_xb)
                r_xo.done(ti, t_d1)
                pt = pbt[sl]
                for c in range(8):
                    OP("tensor", lambda e, c=c, sl=sl, pt=pt: e.transpose(
                        pt[:, c * 128:(c + 1) * 128],
                        xob_stg[sl][:, c * 128:(c + 1) * 128], s_id_bf[:]),
                       waits=([t_xb, r_pbtE.wait(ti)] if c == 0 else []),
                       tick=("pe" if c == 7 else None))
                t_tr3 = S_.count("pe")
                r_xob.done(ti, t_tr3)
                t_cp3 = OP("vector", lambda e, ti=ti, pt=pt: e.tensor_copy(
                    xl1v[:, :, 1 + ti * 128:1 + (ti + 1) * 128],
                    pt[:, 0:1024].rearrange("p (c t) -> p c t", c=8)),
                   waits=[t_tr3], tick="dve")
                r_pbtE.done(ti, t_cp3)
            else:
                if ti >= 1:
                    t_op = OP("gpsimd", lambda e, ti=ti, sl=sl: e.dma_start(
                        opstg2[sl][:], outp_scr[(ti - 1) * 128:ti * 128, :]),
                        waits=[r_op2.wait(ti), outp_done[0]], tick="d_gp")
                    t_f = OP("vector", lambda e, sl=sl: e.tensor_tensor(
                        xo_stg[sl][:], xo_stg[sl][:], opstg2[sl][:], ALU.add),
                       waits=[t_op, t_xo, r_op2.wait(ti)], tick="dve")
                    t_do = OP("sync", lambda e, ti=ti, sl=sl: e.dma_start(
                        out[(ti - 1) * 128:ti * 128, :], xo_stg[sl][:]),
                        waits=[t_f], tick="d_sp")
                    r_xo.done(ti, t_do)
                    r_op2.done(ti, t_do)
                else:
                    r_xo.done(ti, t_xo)
        S_.barrier(ENG)
        ar.reset(m0)

    topY = []
    ffn_layer(0, T2, T1, qoff=5, xin_st=XST,
              in_scr=xn_scr, in_off=5, out_kind="x1", scr_ready=a_dma_done)
    if "L1" not in DEV_SKIP:
        ffn_layer(1, T1, T0E, qoff=4, xin_st=AST,
                  in_scr=x1_scr, in_off=4, out_kind="final", scr_ready=None)

    last = S_.count("d_sp")
    S_.op("sync", "NOP", [last], "fin")

    ar.finalize()
    with nc.Block() as blk:
        S_.replay(blk, ENG)
    global LAST_SCHED
    LAST_SCHED = S_
    return nc



LAST_SCHED = None
_CACHE = {}


def kernel(**inputs):
    x = np.asarray(inputs["L0_attn_x"], np.float32)
    shared = _prep_shared(inputs)
    in_maps = []
    for c in range(NCORE):
        m = dict(shared)
        m.update(_prep_core(x, c // 4, c % 4))
        in_maps.append(m)
    if "nc" not in _CACHE:
        _CACHE["nc"] = build_program()
    res = run_bass_kernel_spmd(_CACHE["nc"], in_maps, core_ids=list(range(NCORE)))
    parts = [np.asarray(r["out"], np.float32) for r in res.results]
    return np.stack([np.concatenate(parts[b * 4:(b + 1) * 4], axis=0)
                     for b in range(B)], axis=0)


if __name__ == "__main__":
    nc = build_program()
    n = sum(1 for _ in nc.all_instructions())
    print("built ok, instructions:", n)


# version 13
# speedup vs baseline: 1.0682x; 1.0682x over previous
import numpy as np
import ml_dtypes
import concourse.bass as bass
import concourse.mybir as mybir
from concourse.bass_utils import run_bass_kernel_spmd

F32 = mybir.dt.float32
BF16 = mybir.dt.bfloat16
AF = mybir.ActivationFunctionType
ALU = mybir.AluOpType
AX = mybir.AxisListType
BF = ml_dtypes.bfloat16

import os
DEV_SKIP = set(os.environ.get("KDEV_SKIP", "").split(","))

B, S, D = 2, 8192, 1024
NCORE = 8
OWN = 2048
CH = 64
WIN = 512
HID = 2730
HIDP = 2816
F1 = 5632
NEG = -1.0e30

R2, R1, R0E = 3328, 2688, 2176
T2, T1, T0E = 26, 21, 17
XC = R2 + S
QROW0 = R2
KT = S // 128
SLAB = 8


def _bf(a):
    return np.ascontiguousarray(np.asarray(a)).astype(BF)



class Sched:
    def __init__(self, nc):
        self.nc = nc
        self.ops = []
        self.sems = {}

    def _sem(self, name):
        if name not in self.sems:
            self.sems[name] = [self.nc.alloc_semaphore(name), 0]
        return self.sems[name]

    def op(self, eng, fn, waits=(), tick=None):
        tk = None
        if tick is not None:
            s = self._sem(tick)
            amt = 16 if tick.startswith("d_") else 1
            s[1] += amt
            tk = (tick, s[1] if False else amt)
        ws = []
        for w in waits:
            if w is None:
                continue
            if isinstance(w, list):
                ws.extend(x for x in w if x is not None)
            else:
                ws.append(w)
        self.ops.append((eng, fn, ws, (tick, tk[1]) if tk else None))
        return (tick, self.sems[tick][1]) if tick else None

    def count(self, name):
        return (name, self.sems[name][1]) if name in self.sems else None

    def barrier(self, engines):
        marks = {}
        for e in engines:
            nm = "bar_" + e
            s = self._sem(nm)
            s[1] += 1
            marks[e] = s[1]
            self.ops.append((e, "NOP", [], (nm, 1)))
        comp = [self.count(nm) for nm in
                ("pe", "dve", "act", "gpc", "d_sp", "d_gp")]
        comp = [c for c in comp if c is not None]
        for e in engines:
            self.ops.append((e, None,
                             [("bar_" + o, marks[o]) for o in engines if o != e]
                             + comp, None))

    def replay(self, blk, engines):
        allwait = "SELFWAIT" in DEV_SKIP
        selfwait_eng = {"vector", "scalar"}
        own_sems = {"sync": ["d_sp"], "gpsimd": ["d_gp", "gpc"],
                    "tensor": ["pe"], "vector": ["dve"], "scalar": ["act"]}

        def engine_body(ename):
            def body(e):
                waited = {}
                counts = {}
                for (eng, fn, waits, tk) in self.ops:
                    if eng != ename:
                        continue
                    if allwait or ename in selfwait_eng:
                        for sn in own_sems[ename]:
                            c = counts.get(sn, 0)
                            if c > waited.get(sn, 0) and sn in self.sems:
                                e.wait_ge(self.sems[sn][0], c)
                                waited[sn] = c
                    for (sn, val) in waits:
                        if waited.get(sn, 0) >= val:
                            continue
                        e.wait_ge(self.sems[sn][0], val)
                        waited[sn] = val
                    if fn == "NOP":
                        instr = e.engine_nop() if hasattr(e, "engine_nop") else e.nop()
                    elif fn is not None:
                        instr = fn(e)
                    else:
                        instr = None
                    if tk is not None and instr is not None:
                        instr.then_inc(self.sems[tk[0]][0], tk[1])
                        counts[tk[0]] = counts.get(tk[0], 0) + tk[1]
                    elif tk is not None:
                        raise RuntimeError("tick with no instruction")
            return body
        for ename in engines:
            getattr(blk, ename)(engine_body(ename))


class Ring:

    def __init__(self, n):
        self.n = n
        self.t = [[] for _ in range(n)]
        self.gen = [None] * n

    def wait(self, i):
        return self.t[i % self.n]

    def done(self, i, tick):
        s = i % self.n
        if self.gen[s] != i:
            self.t[s] = []
            self.gen[s] = i
        if tick is not None:
            self.t[s] = self.t[s] + [tick]


class View:

    def __init__(self, big, off_bytes, shape, dtype):
        self.big = big
        self.off = off_bytes
        self.shape = shape
        self.dtype = dtype
        self.esz = mybir.dt.size(dtype)

    def ap(self):
        a = self.big.ap().bitcast(self.dtype)
        a = a[0:self.shape[0], self.off // self.esz: self.off // self.esz + self.shape[1]]
        return a

    def __getitem__(self, idx):
        return self.ap()[idx]


class Arena:

    def __init__(self, nc, limit=206000):
        self.nc = nc
        self.off = 0
        self.top = limit
        self.limit = limit
        self.big = nc.alloc_sbuf_tensor("arena", [128, limit], mybir.dt.uint8)

    registry = {}

    def alloc(self, name, shape, dtype, align=32):
        per_part = int(np.prod(shape[1:])) * mybir.dt.size(dtype)
        self.off = (self.off + align - 1) // align * align
        v = View(self.big, self.off, [shape[0], int(np.prod(shape[1:]))], dtype)
        Arena.registry[name] = (self.off, shape, dtype)
        self.off += per_part
        assert self.off <= self.top, f"SBUF overflow at {name}: {self.off} > top {self.top}"
        return v

    def alloc_top(self, name, shape, dtype, align=32):
        per_part = int(np.prod(shape[1:])) * mybir.dt.size(dtype)
        newtop = (self.top - per_part) // align * align
        assert newtop >= self.off, f"SBUF overflow(top) at {name}"
        self.top = newtop
        Arena.registry[name] = (newtop, shape, dtype)
        return View(self.big, self.top, [shape[0], int(np.prod(shape[1:]))], dtype)

    def mark(self):
        return self.off

    def reset(self, mark):
        self.off = mark

    def mark_top(self):
        return self.top

    def reset_top(self, mark):
        self.top = mark

    def finalize(self):
        pass



def _prep_shared(inp):
    w = {}
    w["ident_bf"] = _bf(np.eye(128, dtype=np.float32))
    w["ident_f32"] = np.eye(128, dtype=np.float32)
    w["ones_bf"] = _bf(np.ones((128, 128), np.float32))
    kr = np.arange(128)[:, None]
    qr = np.arange(128)[None, :]
    tri = np.zeros((128, 2, 128), np.float32)
    d0 = qr - kr + 512
    d4 = qr - kr
    tri[:, 0, :] = np.where((d0 >= 0) & (d0 <= 512), 0.0, NEG)
    tri[:, 1, :] = np.where((d4 >= 0) & (d4 <= 512), 0.0, NEG)
    w["triT"] = _bf(np.ascontiguousarray(tri.reshape(128, 256)))
    w["ident_m8"] = np.eye(128, dtype=np.float32) * -8.0

    Wqkv = np.asarray(inp["L0_attn_W_qkv"], np.float32)
    qrows = np.concatenate([np.arange(h * 192, h * 192 + 64) for h in range(16)])
    w["Wq_t"] = np.ascontiguousarray(Wqkv[qrows].T)
    w["Wk_t"] = np.ascontiguousarray(Wqkv[qrows + 64].T)
    w["Wv_t"] = np.ascontiguousarray(Wqkv[qrows + 128].T)
    w["Wout_t"] = np.ascontiguousarray(np.asarray(inp["L0_attn_W_out"], np.float32).T)
    w["b_out"] = np.ascontiguousarray(np.asarray(inp["L0_attn_b_out"], np.float32)[None, :])
    for l in range(2):
        w[f"wqkv{l}"] = np.ascontiguousarray(np.asarray(inp["L0_ffn_wqkv"], np.float32)[l])
        w[f"wo{l}"] = np.ascontiguousarray(np.asarray(inp["L0_ffn_wo"], np.float32)[l])
        w1 = np.asarray(inp["L0_ffn_w1"], np.float32)[l]
        b1 = np.asarray(inp["L0_ffn_b1"], np.float32)[l]
        w2 = np.asarray(inp["L0_ffn_w2"], np.float32)[l]
        w1p = np.zeros((D, F1), np.float32)
        b1p = np.zeros((F1,), np.float32)
        for i in range(HIDP // 128):
            if i * 128 < HID:
                n = min(128, HID - i * 128)
                w1p[:, 2 * i * 128:2 * i * 128 + n] = w1[:, i * 128:i * 128 + n]
                w1p[:, (2 * i + 1) * 128:(2 * i + 1) * 128 + n] = w1[:, HID + i * 128:HID + i * 128 + n]
                b1p[2 * i * 128:2 * i * 128 + n] = b1[i * 128:i * 128 + n]
                b1p[(2 * i + 1) * 128:(2 * i + 1) * 128 + n] = b1[HID + i * 128:HID + i * 128 + n]
        w2p = np.zeros((HIDP, D), np.float32)
        w2p[:HID] = w2
        w[f"w1p{l}"] = np.ascontiguousarray(w1p)
        w[f"b1p{l}"] = np.ascontiguousarray(b1p.reshape(F1 // 128, 128).T)
        w[f"w2p{l}"] = np.ascontiguousarray(w2p)
        w[f"b2_{l}"] = np.ascontiguousarray(np.asarray(inp["L0_ffn_b2"], np.float32)[l][None, :])
        w[f"lng_row{l}"] = _bf(np.asarray(inp["L0_ffn_ln_g"], np.float32)[l][None, :])
        w[f"lnb_col{l}"] = np.ascontiguousarray(
            np.asarray(inp["L0_ffn_ln_b"], np.float32)[l].reshape(8, 128).T)
        w[f"qs{l}"] = np.ascontiguousarray(np.asarray(inp["L0_ffn_q_scale"], np.float32)[l][None, :])
        w[f"ks{l}"] = np.ascontiguousarray(np.asarray(inp["L0_ffn_k_scale"], np.float32)[l][None, :])
    return w


def _prep_core(x, b, j):
    s0 = j * OWN
    xb = np.asarray(x[b], np.float32)
    r2abs = s0 - 1280
    sect_f = np.zeros((R2, D), np.float32)
    lo = max(0, r2abs)
    sect_f[lo - r2abs:, :] = xb[lo:s0 + OWN]
    xcore = np.concatenate([sect_f, np.roll(xb, -s0, axis=0)], axis=0)

    t = np.arange(r2abs, r2abs + R2, dtype=np.float64)
    half = np.arange(0, CH, 2, dtype=np.float64)
    inv = 1.0 / 10000.0 ** (half / CH)
    fr32 = (t[:, None].astype(np.float32) * inv[None, :].astype(np.float32))
    freqs = np.concatenate([fr32, fr32], axis=1).astype(np.float64)
    base = (half + 0.4 * CH) / (1.4 * CH)
    power = (t - s0) / float(WIN // 2)
    scale = base[None, :] ** power[:, None]
    scale = np.concatenate([scale, scale], axis=1)
    cos, sin = np.cos(freqs), np.sin(freqs)
    sgn = np.concatenate([-np.ones(32), np.ones(32)])[None, :]
    valid = (t >= 0).astype(np.float32)
    return {
        "xcore": xcore,
        "cq": (cos * scale).astype(np.float32),
        "sq": (sin * scale * sgn).astype(np.float32),
        "ck": (cos / scale).astype(np.float32),
        "sk": (sin / scale * sgn).astype(np.float32),
        "vld": np.ascontiguousarray(valid.reshape(T2, 128).T),
    }



def build_program():
    nc = bass.Bass("TRN2", target_bir_lowering=False, debug=False)
    ar = Arena(nc)
    S_ = Sched(nc)
    ENG = ["sync", "gpsimd", "tensor", "vector", "scalar"]

    dr = {}
    def din(name, shape, dtype=F32):
        dr[name] = nc.dram_tensor(name, list(shape), dtype, kind="ExternalInput")
        return dr[name]

    din("xcore", [XC, D])
    din("Wq_t", [D, D]); din("Wk_t", [D, D]); din("Wv_t", [D, D])
    din("Wout_t", [D, D]); din("b_out", [1, D])
    for l in range(2):
        din(f"wqkv{l}", [D, 1536]); din(f"wo{l}", [512, D])
        din(f"w1p{l}", [D, F1]); din(f"b1p{l}", [128, F1 // 128])
        din(f"w2p{l}", [HIDP, D]); din(f"b2_{l}", [1, D])
        din(f"lng_row{l}", [1, D], BF16); din(f"lnb_col{l}", [128, 8])
        din(f"qs{l}", [1, CH]); din(f"ks{l}", [1, CH])
    din("cq", [R2, CH]); din("sq", [R2, CH]); din("ck", [R2, CH]); din("sk", [R2, CH])
    din("vld", [128, T2])
    din("ident_bf", [128, 128], BF16); din("ident_f32", [128, 128])
    din("ones_bf", [128, 128], BF16); din("triT", [128, 256], BF16)
    din("ident_m8", [128, 128])

    out = nc.dram_tensor("out", [OWN, D], F32, kind="ExternalOutput")

    k_scr = nc.dram_tensor("k_scr", [D, S], BF16, kind="Internal")
    v_scr = nc.dram_tensor("v_scr", [8, 128, KT * 130], BF16, kind="Internal")
    xn_scr = nc.dram_tensor("xn_scr", [R2, D], F32, kind="Internal")
    outp_scr = nc.dram_tensor("outp_scr", [OWN, D], F32, kind="Internal")
    a_scr = nc.dram_tensor("a_scr", [T1 * 128, D], F32, kind="Internal")
    x1_scr = nc.dram_tensor("x1_scr", [T1 * 128, D], F32, kind="Internal")
    x2_scr = nc.dram_tensor("x2_scr", [T1 * 128, D], F32, kind="Internal")
    if "TAP2" in DEV_SKIP:
        tap_wo = nc.dram_tensor("tap_wo", [2, 128, 4 * D], BF16, kind="Internal")
        tap_q = nc.dram_tensor("tap_q", [128, 4 * T2 * 128], BF16, kind="Internal")
        tap_k = nc.dram_tensor("tap_k", [128, 4 * T2 * 128], BF16, kind="Internal")
        tap_v = nc.dram_tensor("tap_v", [128, T2 * 520], BF16, kind="Internal")
        tap_o = nc.dram_tensor("tap_o", [128, 4 * T1 * 128], BF16, kind="Internal")
    if "TAP" in DEV_SKIP:
        tap_scr = nc.dram_tensor("tap_scr", [16, 128, 4 * D], BF16, kind="Internal")
        tap0_scr = nc.dram_tensor("tap0_scr", [16, 128, D], BF16, kind="Internal")
        tapK_scr = nc.dram_tensor("tapK_scr", [16, 128, D], BF16, kind="Internal")
        tapV_scr = nc.dram_tensor("tapV_scr", [16, 128, D], BF16, kind="Internal")
    hq_scr = nc.dram_tensor("hq_scr", [8, 128, T1 * 128], BF16, kind="Internal")

    pbig = [nc.alloc_psum_tensor(f"pbig{i}", [128, 1024], F32) for i in range(2)]
    pb = [pbig[0][:, 0:512], pbig[0][:, 512:1024],
          pbig[1][:, 0:512], pbig[1][:, 512:1024]] + \
         [nc.alloc_psum_tensor(f"pb{i}", [128, 512], F32) for i in range(4, 6)]
    pbt = [nc.alloc_psum_tensor(f"pbt{i}", [128, 1024], BF16) for i in range(2)]

    s_id_bf = ar.alloc("s_id_bf", [128, 128], BF16)
    s_id_f = ar.alloc("s_id_f", [128, 128], F32)
    s_ones = ar.alloc("s_ones", [128, 128], BF16)
    s_tri = ar.alloc("s_tri", [128, 256], BF16)
    s_idm8 = ar.alloc("s_idm8", [128, 128], F32)
    s_vld = ar.alloc("s_vld", [128, T2], F32)
    dmp = ar.alloc("dmpG", [128, D], F32)

    def OP(eng, fn, waits=(), tick=None):
        return S_.op(eng, fn, waits, tick)

    def load(dst, src, q="sync", slow=False):
        t = "d_gp" if q == "gpsimd" else "d_sp"
        if slow:
            return OP(q, lambda e, d=dst, s=src: e.dma_start(
                d, s, allow_slow_non_contiguous=True), tick=t)
        return OP(q, lambda e, d=dst, s=src: e.dma_start(d, s), tick=t)

    c_id = load(s_id_bf[:], dr["ident_bf"][:])
    c_idf = load(s_id_f[:], dr["ident_f32"][:])
    c_ones = load(s_ones[:], dr["ones_bf"][:])
    c_tri = load(s_tri[:], dr["triT"][:])
    c_im8 = load(s_idm8[:], dr["ident_m8"][:])
    c_vld = load(s_vld[:], dr["vld"][:])

    bcast_guard = [None]

    def bcast_row_to_tile(row_ap, dst_ap, ncols, waits, bank, lhs=None):
        lh = lhs if lhs is not None else s_ones[0:1, :]
        t1 = OP("tensor", lambda e, lh=lh, r=row_ap, b=bank, n=ncols: e.matmul(
            pb[b][:, 0:n], lh, r, start=True, stop=True),
            waits=list(waits) + [c_ones, bcast_guard[0]], tick="pe")
        t2 = OP("vector", lambda e, d=dst_ap, b=bank, n=ncols: e.tensor_copy(
            d, pb[b][:, 0:n]), waits=[t1], tick="dve")
        bcast_guard[0] = t2
        return t2

    topX = ar.mark_top()
    XST = R2 + 8
    xnff = ar.alloc_top("xnff", [128, 8 * XST], BF16)
    topQ = ar.mark_top()
    q_fm = ar.alloc_top("q_fm", [128, 8 * OWN], BF16)

    mA = ar.mark()
    xt = [ar.alloc(f"xt{i}", [128, D], F32) for i in range(3)]
    xnb = [ar.alloc(f"xnb{i}", [128, D], BF16) for i in range(2)]
    xnf32 = [ar.alloc(f"xnf32_{i}", [128, D], F32) for i in range(2)]
    st = [ar.alloc(f"stA{i}", [128, 8], F32) for i in range(2)]
    kstg = [ar.alloc(f"kstg{i}", [128, 512], BF16) for i in range(2)]
    vstg = [ar.alloc(f"vstg{i}", [128, 16 * 65], BF16) for i in range(2)]
    xq4 = [ar.alloc(f"xq4_{i}", [128, 4 * D], BF16) for i in range(2)]
    wA = ar.alloc("wA", [128, 8 * 3 * D], BF16)

    for ci in range(8):
        for wi, nm in enumerate(["Wq_t", "Wk_t", "Wv_t"]):
            load(wA[:, (ci * 3 + wi) * D:(ci * 3 + wi + 1) * D],
                 dr[nm][ci * 128:(ci + 1) * 128, :], q="gpsimd")
    wA_all = S_.count("d_gp")

    OP("vector", lambda e: e.memset(
        xnff.ap().rearrange("p (c t) -> p c t", c=8, t=XST)[:, :, 0:1], 0.0),
       tick="dve")

    r_xt = Ring(3)
    r_xnb = Ring(2)
    r_xnf32 = Ring(2)
    r_st = Ring(2)
    r_pbt = Ring(2)
    r_kbank = Ring(2)
    r_vbank = Ring(2)
    r_qbank = Ring(2)
    r_kstg = Ring(2)
    r_vstg = Ring(2)

    def ln_tile(idx, row, want_f32):
        xs = idx % 3
        ss = st[idx % 2][:]
        t_ld = OP("sync", lambda e, xs=xs, row=row: e.dma_start(
            xt[xs][:], dr["xcore"][row:row + 128, :]),
            waits=[r_xt.wait(idx)], tick="d_sp")
        t_sum = OP("vector", lambda e, xs=xs, ss=ss: e.tensor_scalar(
            dmp[:], xt[xs][:], 1.0, None, ALU.mult, ALU.add,
            accum_out=ss[:, 0:1]), waits=[t_ld, r_st.wait(idx)], tick="dve")
        t_ssq = OP("scalar", lambda e, xs=xs, ss=ss, i=idx % 2: e.activation(
            xnb[i][:], xt[xs][:], AF.Square, accum_out=ss[:, 1:2]),
            waits=[t_ld, r_st.wait(idx), t_sum, r_xnb.wait(idx)], tick="act")
        OP("vector", lambda e, ss=ss: e.tensor_scalar(
            ss[:, 2:3], ss[:, 0:1], 1.0 / D, None, ALU.mult, ALU.bypass),
           waits=[t_ssq], tick="dve")
        OP("vector", lambda e, ss=ss: e.tensor_scalar(
            ss[:, 3:4], ss[:, 2:3], ss[:, 2:3], None, ALU.mult, ALU.bypass),
           tick="dve")
        OP("vector", lambda e, ss=ss: e.tensor_scalar(
            ss[:, 4:5], ss[:, 1:2], 1.0 / D, ss[:, 3:4], ALU.mult, ALU.subtract),
           tick="dve")
        t_v = OP("vector", lambda e, ss=ss: e.tensor_scalar(
            ss[:, 4:5], ss[:, 4:5], 0.0, 1e-5, ALU.max, ALU.add), tick="dve")
        t_sq = OP("scalar", lambda e, ss=ss: e.activation(
            ss[:, 5:6], ss[:, 4:5], AF.Sqrt), waits=[t_v], tick="act")
        t_r = OP("vector", lambda e, ss=ss: e.reciprocal(ss[:, 6:7], ss[:, 5:6]),
                 waits=[t_sq], tick="dve")
        if want_f32:
            t_f = OP("vector", lambda e, xs=xs, ss=ss, i=idx % 2: e.tensor_scalar(
                xnf32[i][:], xt[xs][:], ss[:, 2:3], ss[:, 6:7],
                ALU.subtract, ALU.mult),
                waits=[t_r, r_xnf32.wait(idx)], tick="dve")
            t_b = OP("scalar", lambda e, i=idx % 2: e.copy(xnb[idx % 2][:], xnf32[i][:]),
                     waits=[t_f, r_xnb.wait(idx)], tick="act")
        else:
            t_b = OP("vector", lambda e, xs=xs, ss=ss: e.tensor_scalar(
                xnb[idx % 2][:], xt[xs][:], ss[:, 2:3], ss[:, 6:7],
                ALU.subtract, ALU.mult),
                waits=[t_r, r_xnb.wait(idx)], tick="dve")
            t_f = None
        r_st.done(idx, t_b)
        r_xt.done(idx, t_b)
        return t_b, t_f

    def tr_tile(idx, src_bf_ap, dst_view, wait, dst_free=None):
        pt = pbt[idx % 2]
        for c in range(8):
            OP("tensor", lambda e, c=c, pt=pt, s=src_bf_ap: e.transpose(
                pt[:, c * 128:(c + 1) * 128],
                s[:, c * 128:(c + 1) * 128], s_id_bf[:]),
               waits=([wait, c_id, r_pbt.wait(idx)] if c == 0 else []),
               tick=("pe" if c == 7 else None))
        t_tr = S_.count("pe")
        t_cp = OP("vector", lambda e, d=dst_view, pt=pt: e.tensor_copy(
            d, pt[:, 0:1024].rearrange("p (c t) -> p c t", c=8)),
            waits=[t_tr, dst_free], tick="dve")
        r_pbt.done(idx, t_cp)
        return t_cp

    xnffv = xnff.ap().rearrange("p (c t) -> p c t", c=8, t=XST)

    for ti in range(T2):
        t_b, t_f = ln_tile(ti, ti * 128, want_f32=True)
        t_st = OP("sync", lambda e, ti=ti, i=ti % 2: e.dma_start(
            xn_scr[ti * 128:(ti + 1) * 128, :], xnf32[i][:]),
            waits=[t_f], tick="d_sp")
        r_xnf32.done(ti, t_st)
        t_cp = tr_tile(ti, xnb[ti % 2][:], xnffv[:, :, 1 + ti * 128:1 + (ti + 1) * 128], t_b)
        r_xnb.done(ti, t_cp)

    r_xq4 = Ring(2)
    for g in range(16):
        gs = 4 * g
        slot = g % 2
        xgv = xq4[slot].ap().rearrange("p (c t) -> p c t", c=8, t=512)
        t_last = None
        for i in range(4):
            ti = gs + i
            idx = T2 + ti
            t_b, _ = ln_tile(idx, R2 + ti * 128, want_f32=False)
            t_cp = tr_tile(idx, xnb[idx % 2][:], xgv[:, :, i * 128:(i + 1) * 128], t_b,
                           dst_free=r_xq4.wait(g))
            r_xnb.done(idx, t_cp)
            t_last = t_cp
            if "TAP" in DEV_SKIP and i == 0:
                OP("sync", lambda e, g=g, slot=slot: e.dma_start(
                    tap0_scr.ap()[g:g + 1, :, :].squeeze(0)
                    .rearrange("p (c t) -> p c t", c=8),
                    xq4[slot].ap().rearrange("p (c t) -> p c t", c=8, t=512)[:, :, 0:128]),
                   waits=[t_cp], tick="d_sp")
        for oc in range(8):
            bank = oc % 2
            for ci in range(8):
                OP("tensor", lambda e, oc=oc, ci=ci, bank=bank, xgv=xgv: e.matmul(
                    pb[bank][:, 0:512],
                    wA[:, (ci * 3 + 1) * D + oc * 128:(ci * 3 + 1) * D + oc * 128 + 128],
                    xgv[:, ci, :], start=(ci == 0), stop=(ci == 7)),
                   waits=([t_last, wA_all, r_kbank.wait(oc)] if ci == 0 else []),
                   tick=("pe" if ci == 7 else None))
            t_k = S_.count("pe")
            t_ev = OP("vector", lambda e, bank=bank, oc=oc: e.tensor_copy(
                kstg[oc % 2][:], pb[bank][:, 0:512]),
                waits=[t_k, r_kstg.wait(oc)], tick="dve")
            r_kbank.done(oc, t_ev)
            t_dk = OP("sync", lambda e, oc=oc, gs=gs: e.dma_start(
                k_scr[oc * 128:(oc + 1) * 128, gs * 128:gs * 128 + 512],
                kstg[oc % 2][:]), waits=[t_ev], tick="d_sp")
            r_kstg.done(oc, t_dk)
        if "TAP" in DEV_SKIP:
            OP("sync", lambda e, g=g, slot=slot: e.dma_start(
                tapK_scr.ap()[g:g + 1, :, :].squeeze(0)
                .rearrange("p (c t) -> p c t", c=8),
                xq4[slot].ap().rearrange("p (c t) -> p c t", c=8, t=512)[:, :, 0:128]),
               waits=[S_.count("d_sp"), S_.count("dve"), S_.count("pe")], tick="d_sp")
        for i in range(4 if "V" not in DEV_SKIP else 0):
            for half in range(2):
                bank = 2 + half
                for ci in range(8):
                    OP("tensor", lambda e, i=i, half=half, ci=ci, bank=bank, xgv=xgv:
                       e.matmul(
                        pb[bank][:, 0:512],
                        xgv[:, ci, i * 128:(i + 1) * 128],
                        wA[:, (ci * 3 + 2) * D + half * 512:
                           (ci * 3 + 2) * D + half * 512 + 512],
                        start=(ci == 0), stop=(ci == 7)),
                       waits=([r_vbank.wait(half)] if ci == 0 else []),
                       tick=("pe" if (ci == 7) else None))
            t_v = S_.count("pe")
            vs = vstg[i % 2].ap().rearrange("p (h c) -> p h c", h=16)
            OP("vector", lambda e, vs=vs: e.tensor_copy(
                vs[:, 0:8, 0:64], pb[2][:, 0:512].rearrange("p (h c) -> p h c", h=8)),
               waits=[t_v, r_vstg.wait(i)], tick="dve")
            t_e2 = OP("vector", lambda e, vs=vs: e.tensor_copy(
                vs[:, 8:16, 0:64], pb[3][:, 0:512].rearrange("p (h c) -> p h c", h=8)),
               tick="dve")
            r_vbank.done(0, t_e2)
            r_vbank.done(1, t_e2)
            t_vc = OP("vector", lambda e, vs=vs: e.memset(vs[:, :, 64:65], 1.0),
                      tick="dve")
            t_dv = None
            for p8 in range(8):
                t_dv = OP("sync", lambda e, p8=p8, ti=gs + i, i=i: e.dma_start(
                    v_scr[p8, :, ti * 130:(ti + 1) * 130],
                    vstg[i % 2][:, p8 * 130:(p8 + 1) * 130]),
                    waits=[t_vc], tick="d_sp")
            r_vstg.done(i, t_dv)
        if "TAP" in DEV_SKIP:
            OP("sync", lambda e, g=g, slot=slot: e.dma_start(
                tapV_scr.ap()[g:g + 1, :, :].squeeze(0)
                .rearrange("p (c t) -> p c t", c=8),
                xq4[slot].ap().rearrange("p (c t) -> p c t", c=8, t=512)[:, :, 0:128]),
               waits=[S_.count("d_sp"), S_.count("dve"), S_.count("pe")], tick="d_sp")
        if g < 4 and "Q" not in DEV_SKIP:
            qfv = q_fm.ap().rearrange("p (c t) -> p c t", c=8, t=OWN)
            for oc in range(8):
                bank = 4 + (oc % 2)
                for ci in range(8):
                    OP("tensor", lambda e, oc=oc, ci=ci, bank=bank, xgv=xgv: e.matmul(
                        pb[bank][:, 0:512],
                        wA[:, (ci * 3 + 0) * D + oc * 128:(ci * 3 + 0) * D + oc * 128 + 128],
                        xgv[:, ci, :], start=(ci == 0), stop=(ci == 7)),
                       waits=([r_qbank.wait(oc)] if ci == 0 else []),
                       tick=("pe" if ci == 7 else None))
                t_q = S_.count("pe")
                t_eq = OP("vector", lambda e, bank=bank, oc=oc, gs=gs, qfv=qfv:
                          e.tensor_copy(
                              qfv[:, oc, gs * 128:gs * 128 + 512], pb[bank][:, 0:512]),
                          waits=[t_q], tick="dve")
                r_qbank.done(oc, t_eq)
        if "TAP" in DEV_SKIP:
            OP("sync", lambda e, g=g, slot=slot: e.dma_start(
                tap_scr[g, :, :], xq4[slot][:]), waits=[t_last], tick="d_sp")
        r_xq4.done(g, S_.count("pe"))

    S_.barrier(ENG)
    ar.reset(mA)

    mB = ar.mark()
    kbuf = [ar.alloc(f"kbuf{i}", [128, S], BF16) for i in range(2)]
    vbuf = [ar.alloc(f"vbuf{i}", [128, KT * 130], BF16) for i in range(2)]
    pts = [ar.alloc(f"pts{i}", [128, SLAB * 128], BF16) for i in range(2)]
    o_fm = ar.alloc("o_fm", [128, 8 * OWN], BF16)
    ors = ar.alloc("ors", [128, 4], F32)
    otmb = [ar.alloc(f"otmb{i}", [128, 64], BF16) for i in range(2)]

    qfv = q_fm.ap().rearrange("p (c t) -> p c t", c=8, t=OWN)
    ofv = o_fm.ap().rearrange("p (c t) -> p c t", c=8, t=OWN)

    r_kv = Ring(2)
    r_eslab = Ring(2)
    r_pts = Ring(2)
    r_obank = Ring(2)
    r_otmb = Ring(2)
    r_pbtB = Ring(2)

    pair_tick = {}
    a_dma_done = S_.count("d_sp")

    def load_pair(p):
        t1 = OP("sync", lambda e, p=p: e.dma_start(
            kbuf[p % 2][:], k_scr[p * 128:(p + 1) * 128, :]),
            waits=[r_kv.wait(p), a_dma_done], tick="d_sp")
        t2 = OP("gpsimd", lambda e, p=p: e.dma_start(
            vbuf[p % 2][:], v_scr[p, :, :]),
            waits=[r_kv.wait(p), a_dma_done], tick="d_gp")
        pair_tick[p] = [t1, t2]

    load_pair(0)
    load_pair(1)
    nslab = KT // 4
    gslab = 0
    for p in range(8):
        for hh in range(2):
            h = 2 * p + hh
            bs = hh * 64
            for qg in range(8):
                t_pv = None
                pend_pv = None
                for sl in range(nslab):
                    ebig = pbig[gslab % 2]
                    spt = pts[gslab % 2]
                    for i in range(4):
                        kt = sl * 4 + i
                        w = []
                        if i == 0:
                            w = [r_eslab.wait(gslab)] + list(pair_tick[p])
                            if sl == 0:
                                w.append(r_obank.wait(2 * qg))
                                w.append(r_obank.wait(2 * qg + 1))
                        OP("tensor", lambda e, kt=kt, i=i, ebig=ebig, bs=bs, p=p,
                           qg=qg: e.matmul(
                            ebig[:, i * 256:i * 256 + 256],
                            kbuf[p % 2][bs:bs + 64, kt * 128:(kt + 1) * 128],
                            qfv[bs:bs + 64, p, qg * 256:qg * 256 + 256],
                            start=True, stop=True),
                           waits=w, tick=("pe" if i == 3 else None))
                    t_qk = S_.count("pe")
                    t_exp = OP("scalar", lambda e, spt=spt, ebig=ebig: e.activation(
                        spt[:, 0:1024], ebig[:, 0:1024], AF.Exp, scale=1.0 / 32.0),
                        waits=[t_qk, r_pts.wait(gslab)], tick="act")
                    r_eslab.done(gslab, t_exp)

                    def mk_pv(sl=sl, spt=spt, t_exp=t_exp, gs=gslab, hh=hh, p=p):
                        def em():
                            for i in range(4):
                                kt = sl * 4 + i
                                for j in range(2):
                                    OP("tensor", lambda e, kt=kt, i=i, j=j, sl=sl,
                                       spt=spt, hh=hh, p=p: e.matmul(
                                        pb[4 + j][:, 0:65],
                                        spt[:, i * 256 + j * 128:
                                            i * 256 + j * 128 + 128],
                                        vbuf[p % 2][:, kt * 130 + hh * 65:
                                                    kt * 130 + hh * 65 + 65],
                                        start=(sl == 0 and i == 0),
                                        stop=(sl == nslab - 1 and i == 3),
                                        skip_group_check=True),
                                       waits=([t_exp] if (i == 0 and j == 0) else []),
                                       tick=("pe" if (i == 3 and j == 1) else None))
                            t = S_.count("pe")
                            r_pts.done(gs, t)
                            return t
                        return em
                    cur_pv = mk_pv()
                    if pend_pv is not None:
                        t_pv = pend_pv()
                    pend_pv = cur_pv
                    gslab += 1
                t_pv = pend_pv()
                for j in range(2):
                    qt = 2 * qg + j
                    ob = 4 + j
                    OP("vector", lambda e, ob=ob, j=j: e.tensor_scalar(
                        ors[:, 2 * j:2 * j + 1], pb[ob][:, 64:65], 1e-30, None,
                        ALU.max, ALU.bypass),
                       waits=[t_pv], tick="dve")
                    OP("vector", lambda e, j=j: e.reciprocal(
                        ors[:, 2 * j + 1:2 * j + 2], ors[:, 2 * j:2 * j + 1]),
                       tick="dve")
                    t_on = OP("vector", lambda e, ob=ob, j=j: e.tensor_scalar(
                        otmb[j][:], pb[ob][:, 0:64], ors[:, 2 * j + 1:2 * j + 2],
                        None, ALU.mult, ALU.bypass),
                        waits=[r_otmb.wait(qt)], tick="dve")
                    r_obank.done(qt, t_on)
                    OP("tensor", lambda e, bs=bs, j=j: e.transpose(
                        pbt[j][bs:bs + 64, 0:128], otmb[j][:], s_id_bf[:]),
                       waits=[t_on, r_pbtB.wait(qt)], tick="pe")
                    t_otr = S_.count("pe")
                    r_otmb.done(qt, t_otr)
                    t_oc = OP("vector", lambda e, bs=bs, p=p, qt=qt, j=j:
                       e.tensor_copy(
                        ofv[bs:bs + 64, p, qt * 128:(qt + 1) * 128],
                        pbt[j][bs:bs + 64, 0:128]), waits=[t_otr], tick="dve")
                    r_pbtB.done(qt, t_oc)
        if p + 2 < 8:
            r_kv.done(p, S_.count("pe"))
            load_pair(p + 2)

    S_.barrier(ENG)
    ar.reset(mB)

    mB2 = ar.mark()
    s_boutbc = ar.alloc("s_boutbc", [128, D], F32)
    s_bout_bf = ar.alloc("s_bout_bf", [1, D], BF16)
    c_bo = load(s_bout_bf[:], dr["b_out"][:], q="gpsimd")
    tb = bcast_row_to_tile(s_bout_bf[0:1, 0:512], s_boutbc[:, 0:512], 512, [c_bo], 4)
    boutbc_done = bcast_row_to_tile(s_bout_bf[0:1, 512:1024], s_boutbc[:, 512:1024],
                                    512, [c_bo], 4)
    wout_b = ar.alloc("wout_b", [128, 8 * D], BF16)
    xstg = [ar.alloc(f"xstgB{i}", [128, D], F32) for i in range(2)]
    xnstg = [ar.alloc(f"xnstgB{i}", [128, D], F32) for i in range(2)]
    opstg = [ar.alloc(f"opstgB{i}", [128, D], F32) for i in range(2)]
    for ci in range(8):
        load(wout_b[:, ci * D:(ci + 1) * D], dr["Wout_t"][ci * 128:(ci + 1) * 128, :],
             q="gpsimd")
    t_wo = S_.count("d_gp")
    r_b2bank = Ring(4)
    r_b2stg = Ring(2)
    r_bx = Ring(2)
    r_bxn = Ring(2)
    for ti in range(16):
        sl = ti % 2
        t_x = OP("sync", lambda e, ti=ti, sl=sl: e.dma_start(
            xstg[sl][:], dr["xcore"][QROW0 + ti * 128:QROW0 + (ti + 1) * 128, :]),
            waits=[r_bx.wait(ti)], tick="d_sp")
        t_xn = OP("gpsimd", lambda e, ti=ti, sl=sl: e.dma_start(
            xnstg[sl][:], xn_scr[1280 + ti * 128:1280 + (ti + 1) * 128, :]),
            waits=[r_bxn.wait(ti), a_dma_done], tick="d_gp")
        for half in range(2):
            bank = sl * 2 + half
            for ci in range(8):
                OP("tensor", lambda e, ci=ci, half=half, ti=ti, bank=bank: e.matmul(
                    pb[bank][:, 0:512],
                    ofv[:, ci, ti * 128:(ti + 1) * 128],
                    wout_b[:, ci * D + half * 512:ci * D + half * 512 + 512],
                    start=(ci == 0), stop=(ci == 7)),
                   waits=([t_wo, r_b2bank.wait(bank)] if ci == 0 else []),
                   tick=("pe" if ci == 7 else None))
        t_mm = S_.count("pe")
        for half in range(2):
            bank = sl * 2 + half
            hs = slice(half * 512, half * 512 + 512)
            OP("vector", lambda e, bank=bank, hs=hs, sl=sl: e.tensor_tensor(
                opstg[sl][:, hs], pb[bank][:, 0:512], s_boutbc[:, hs], ALU.add),
               waits=[t_mm, boutbc_done, r_b2stg.wait(ti)], tick="dve")
            OP("vector", lambda e, hs=hs, sl=sl: e.tensor_tensor(
                opstg[sl][:, hs], opstg[sl][:, hs], xnstg[sl][:, hs], ALU.add),
               waits=[t_xn], tick="dve")
            t_ep = OP("vector", lambda e, hs=hs, sl=sl: e.tensor_tensor(
                opstg[sl][:, hs], opstg[sl][:, hs], xstg[sl][:, hs], ALU.add),
               waits=[t_x], tick="dve")
            r_b2bank.done(bank, t_ep)
        r_bx.done(ti, t_ep)
        r_bxn.done(ti, t_ep)
        t_d = OP("sync", lambda e, ti=ti, sl=sl: e.dma_start(
            outp_scr[ti * 128:(ti + 1) * 128, :], opstg[sl][:]),
            waits=[t_ep], tick="d_sp")
        r_b2stg.done(ti, t_d)

    outp_done = [S_.count("d_sp")]
    S_.barrier(ENG)
    ar.reset(mB2)
    ar.reset_top(topQ)

    AST = T1 * 128 + 8
    xinfm_l1 = None

    def ffn_layer(l, Ti, To, qoff, xin_st, in_scr, in_off, out_kind, scr_ready):
        nonlocal xinfm_l1
        xin_fm = xnff if l == 0 else xinfm_l1
        m0 = ar.mark()
        xiv = xin_fm.ap().rearrange("p (c t) -> p c t", c=8, t=xin_st)

        qsrow = ar.alloc("qsrow", [1, 4 * CH], BF16)
        t_qs = load(qsrow[0:1, 0:CH], dr[f"qs{l}"][:], q="gpsimd")
        t_ks = load(qsrow[0:1, 2 * CH:3 * CH], dr[f"ks{l}"][:], q="gpsimd")
        OP("vector", lambda e: e.tensor_copy(
            qsrow.ap()[0:1, CH:2 * CH].rearrange("p (a b) -> p a b", a=2),
            qsrow.ap()[0:1, 0:CH].rearrange("p (a b) -> p a b", a=2)[:, ::-1, :]),
           waits=[t_qs], tick="dve")
        t_sr = OP("vector", lambda e: e.tensor_copy(
            qsrow.ap()[0:1, 3 * CH:4 * CH].rearrange("p (a b) -> p a b", a=2),
            qsrow.ap()[0:1, 2 * CH:3 * CH].rearrange("p (a b) -> p a b", a=2)[:, ::-1, :]),
           waits=[t_ks], tick="dve")
        sc_bc = ar.alloc("sc_bc", [128, 4 * CH], BF16)
        t_scbc = None
        for i in range(4):
            t_scbc = bcast_row_to_tile(qsrow[0:1, i * CH:(i + 1) * CH],
                                       sc_bc[:, i * CH:(i + 1) * CH], CH, [t_sr], 4)

        ofm = ar.alloc("ofm", [128, 4 * To * 128], BF16)
        arow = ar.alloc("arow", [1, To * 128], BF16)
        brow = ar.alloc("brow", [1, To * 128], BF16)
        mKV = ar.mark()
        kfm = ar.alloc("kfm", [128, 4 * Ti * 128], BF16)
        qfm2 = ar.alloc("qfm2", [128, 4 * Ti * 128], BF16)
        vaug = ar.alloc("vaug", [128, Ti * 520], BF16)
        mC1 = ar.mark()
        wqkv_b = ar.alloc("wqkv_b", [128, 8 * 1536], BF16)
        for ci in range(8):
            load(wqkv_b[:, ci * 1536:(ci + 1) * 1536],
                 dr[f"wqkv{l}"][ci * 128:(ci + 1) * 128, :], q="gpsimd")
        t_wq = S_.count("d_gp")
        tblf = [ar.alloc("tblf0", [128, 4 * CH], F32)] * 2
        tblb = ar.alloc("tblb", [128, 4 * CH], BF16)
        qk_tm = ar.alloc("qk_tm", [128, 1024], BF16)
        sq_tm = ar.alloc("sq_tm", [128, 1024], BF16)
        u_tm = ar.alloc("u_tm", [128, 512], BF16)
        v1_tm = ar.alloc("v1_tm", [128, 512], BF16)
        rst = [ar.alloc(f"rstC{i}", [128, 48], F32) for i in range(2)]
        kvv = kfm.ap().rearrange("p (c t) -> p c t", c=4)
        qvv = qfm2.ap().rearrange("p (c t) -> p c t", c=4)
        vav = vaug.ap().rearrange("p (t h c) -> p t h c", t=Ti, h=8)

        r_qkvbank = Ring(2)
        r_tbl = Ring(1)
        r_qktm = Ring(1)
        r_rst = Ring(2)
        r_pbtC = Ring(2)

        for ti in range(Ti):
            b0 = (ti % 2) * 3
            for ci in range(8):
                coff = ti * 128 + (1 if ci < 4 else 0)
                for nb in range(3):
                    OP("tensor", lambda e, ci=ci, nb=nb, b0=b0, coff=coff: e.matmul(
                        pb[b0 + nb][:, 0:512],
                        xiv[:, ci, coff:coff + 128],
                        wqkv_b[:, ci * 1536 + nb * 512:ci * 1536 + nb * 512 + 512],
                        start=(ci == 0), stop=(ci == 7)),
                       waits=([t_wq, r_qkvbank.wait(ti)] if (ci == 0 and nb == 0) else []),
                       tick=("pe" if (ci == 7 and nb == 2) else None))
            t_qkv = S_.count("pe")
            OP("vector", lambda e, b0=b0: e.tensor_copy(
                qk_tm[:, 0:512], pb[b0][:, 0:512]),
               waits=[t_qkv, r_qktm.wait(ti - 1)], tick="dve")
            t_qke = OP("vector", lambda e, b0=b0: e.tensor_copy(
                qk_tm[:, 512:1024], pb[b0 + 1][:, 0:512]), tick="dve")
            OP("vector", lambda e, b0=b0, ti=ti: e.tensor_copy(
                vav[:, ti, :, 0:64],
                pb[b0 + 2][:, 0:512].rearrange("p (h c) -> p h c", h=8)),
               tick="dve")
            vcol = ti + (T2 - Ti)
            t_ve = OP("vector", lambda e, ti=ti, vcol=vcol: e.tensor_copy(
                vav[:, ti, :, 64:65],
                s_vld.ap()[:, vcol:vcol + 1].unsqueeze(1).broadcast_to((128, 8, 1))),
               waits=[c_vld], tick="dve")
            r_qkvbank.done(ti, t_ve)
            rr = rst[ti % 2]
            t_sq1 = OP("scalar", lambda e: e.activation(
                sq_tm[:, 0:512], qk_tm[:, 0:512], AF.Square),
                waits=[t_qke, r_rst.wait(ti)], tick="act")
            t_sq2 = OP("scalar", lambda e: e.activation(
                sq_tm[:, 512:1024], qk_tm[:, 512:1024], AF.Square), tick="act")
            OP("vector", lambda e, rr=rr: e.tensor_reduce(
                rr[:, 0:8], sq_tm.ap()[:, 0:512].rearrange("p (h c) -> p h c", h=8),
                AX.X, ALU.add), waits=[t_sq1, t_sq2], tick="dve")
            t_red = OP("vector", lambda e, rr=rr: e.tensor_reduce(
                rr[:, 8:16], sq_tm.ap()[:, 512:1024].rearrange("p (h c) -> p h c", h=8),
                AX.X, ALU.add), tick="dve")
            t_sq3 = OP("scalar", lambda e, rr=rr: e.activation(
                rr[:, 16:32], rr[:, 0:16], AF.Sqrt), waits=[t_red], tick="act")
            OP("vector", lambda e, rr=rr: e.tensor_scalar(
                rr[:, 16:32], rr[:, 16:32], 1e-12, None, ALU.max, ALU.bypass),
               waits=[t_sq3], tick="dve")
            t_rn = OP("vector", lambda e, rr=rr: e.reciprocal(
                rr[:, 32:48], rr[:, 16:32]), tick="dve")
            tsl = ti % 2
            t_tb = []
            for i, nm in enumerate(["cq", "sq", "ck", "sk"]):
                row0 = (ti + (T2 - Ti)) * 128
                t_tb.append(OP("sync", lambda e, i=i, nm=nm, row0=row0, tsl=tsl:
                               e.dma_start(tblf[tsl][:, i * CH:(i + 1) * CH],
                                           dr[nm][row0:row0 + 128, :]),
                               waits=[r_tbl.wait(ti)], tick="d_sp"))
            t_tbl = None
            for i in range(4):
                t_tbl = OP("vector", lambda e, i=i, tsl=tsl: e.tensor_tensor(
                    tblb[:, i * CH:(i + 1) * CH], tblf[tsl][:, i * CH:(i + 1) * CH],
                    sc_bc[:, i * CH:(i + 1) * CH], ALU.mult),
                    waits=[t_tb[i], t_scbc], tick="dve")
            r_tbl.done(ti, t_tbl)
            t_tr_last = None
            for side in range(2):
                for hh2 in range(8):
                    OP("vector", lambda e, side=side, hh2=hh2, rr=rr: e.tensor_scalar(
                        u_tm[:, hh2 * CH:(hh2 + 1) * CH],
                        qk_tm[:, side * 512 + hh2 * CH:side * 512 + hh2 * CH + CH],
                        rr[:, 32 + side * 8 + hh2:32 + side * 8 + hh2 + 1], None,
                        ALU.mult, ALU.bypass),
                       waits=([t_rn] if hh2 == 0 else []),
                       tick=("dve" if hh2 == 7 else None))
                OP("vector", lambda e, side=side: e.tensor_tensor(
                    v1_tm.ap()[:, 0:512].rearrange("p (h c) -> p h c", h=8),
                    u_tm.ap()[:, 0:512].rearrange("p (h c) -> p h c", h=8),
                    tblb.ap()[:, (side * 2) * CH:(side * 2) * CH + CH]
                        .unsqueeze(1).broadcast_to((128, 8, CH)),
                    ALU.mult), waits=[t_tbl], tick="dve")
                OP("vector", lambda e, side=side: e.tensor_tensor(
                    u_tm.ap()[:, 0:512].rearrange("p (h r c) -> p h r c", h=8, r=2),
                    u_tm.ap()[:, 0:512].rearrange("p (h r c) -> p h r c", h=8, r=2)[
                        :, :, ::-1, :],
                    tblb.ap()[:, (side * 2 + 1) * CH:(side * 2 + 1) * CH + CH]
                        .rearrange("p (r c) -> p r c", r=2)
                        .unsqueeze(1).broadcast_to((128, 8, 2, 32)),
                    ALU.mult), tick="dve")
                t_v2 = OP("vector", lambda e, side=side: e.tensor_tensor(
                    qk_tm[:, side * 512:side * 512 + 512], v1_tm[:, 0:512],
                    u_tm[:, 0:512], ALU.add), tick="dve")
                pt = pbt[(2 * ti + side) % 2]
                for bc in range(4):
                    OP("tensor", lambda e, side=side, bc=bc, pt=pt: e.transpose(
                        pt[:, bc * 128:(bc + 1) * 128],
                        qk_tm[:, side * 512 + bc * 128:side * 512 + bc * 128 + 128],
                        s_id_bf[:]),
                       waits=([t_v2, r_pbtC.wait(2 * ti + side)] if bc == 0 else []),
                       tick=("pe" if bc == 3 else None))
                t_tr = S_.count("pe")
                dst = (qvv if side == 0 else kvv)
                t_cp = OP("vector", lambda e, dst=dst, ti=ti, pt=pt: e.tensor_copy(
                    dst[:, :, ti * 128:(ti + 1) * 128],
                    pt[:, 0:512].rearrange("p (c t) -> p c t", c=4)),
                    waits=[t_tr], tick="dve")
                r_pbtC.done(2 * ti + side, t_cp)
                t_tr_last = t_cp
            r_qktm.done(ti, t_tr_last)
            r_qktm.done(ti, t_sq2)
            r_rst.done(ti, t_tr_last)
        if "TAP2" in DEV_SKIP and l == 0:
            tw = [S_.count("dve"), S_.count("pe"), S_.count("act")]
            OP("sync", lambda e: e.dma_start(tap_q[:, 0:4 * Ti * 128], qfm2[:]), waits=tw, tick="d_sp")
            OP("sync", lambda e: e.dma_start(tap_k[:, 0:4 * Ti * 128], kfm[:]), waits=tw, tick="d_sp")
            OP("sync", lambda e: e.dma_start(tap_v[:, 0:Ti * 520], vaug[:]), waits=tw, tick="d_sp")
        S_.barrier(ENG)
        ar.reset(mC1)
        if l == 0:
            ar.reset_top(topX)
        else:
            ar.reset_top(topY[0])

        ofv2 = ofm.ap().rearrange("p (c t) -> p c t", c=4)
        mcol = ar.alloc("mcol", [128, 2], F32)
        crow = ar.alloc("crow", [1, 4], BF16)
        biasc = ar.alloc("biasc", [128, 2], F32)
        ptl = [ar.alloc(f"ptl{i}", [128, 640], BF16) for i in range(2)]
        otm2 = [ar.alloc(f"otm2_{i}", [128, 64], BF16) for i in range(2)]
        ors2 = ar.alloc("ors2", [128, 4], F32)

        r_el = Ring(2)
        r_ol = Ring(2)
        r_ptl = Ring(2)
        r_mcol = Ring(2)
        r_crow = Ring(2)
        r_biasc = Ring(2)
        r_otm2 = Ring(2)
        r_pbtL = Ring(2)
        exp_tick = {}

        r_ors = Ring(1)
        its = [(h, h // 2, (h % 2) * 64, qt) for h in range(8) for qt in range(To)]
        N2 = len(its)
        st2 = [dict() for _ in range(N2)]

        def c2_s0(it):
            h, hp, bs, qt = its[it]
            sl = it % 2
            e0, e1 = pb[sl * 2], pb[sl * 2 + 1]
            g0 = qt + qoff - 4
            for gi in range(5):
                g = g0 + gi
                eb, ecol = (e0, gi * 128) if gi < 4 else (e1, 0)
                masked = gi in (0, 4)
                OP("tensor", lambda e, g=g, eb=eb, ecol=ecol, hp=hp, bs=bs, qt=qt,
                   masked=masked: e.matmul(
                    eb[:, ecol:ecol + 128],
                    kvv[bs:bs + 64, hp, g * 128:(g + 1) * 128],
                    qvv[bs:bs + 64, hp, (qt + qoff) * 128:(qt + qoff) * 128 + 128],
                    start=True, stop=not masked),
                   waits=([r_el.wait(it)] if gi == 0 else []), tick=None)
                if masked:
                    OP("tensor", lambda e, eb=eb, gi=gi: e.matmul(
                        eb[:, 0:128], s_id_bf[:],
                        s_tri[:, 0:128] if gi == 0 else s_tri[:, 128:256],
                        start=False, stop=True),
                       waits=[c_tri, c_id], tick=("pe" if gi == 4 else None))
            st2[it]["qk"] = S_.count("pe")

        def c2_sa(it):
            h, hp, bs, qt = its[it]
            sl = it % 2
            e0, e1 = pb[sl * 2], pb[sl * 2 + 1]
            OP("vector", lambda e, e0=e0, sl=sl: e.tensor_reduce(
                mcol[:, sl:sl + 1], e0[:, 0:512], AX.X, ALU.max),
               waits=[st2[it]["qk"], r_mcol.wait(it)], tick="dve")
            OP("vector", lambda e, e1=e1, sl=sl: e.tensor_reduce(
                ors2[:, 2 + sl:3 + sl], e1[:, 0:128], AX.X, ALU.max), tick="dve")
            t_mc = OP("vector", lambda e, sl=sl: e.tensor_tensor(
                mcol[:, sl:sl + 1], mcol[:, sl:sl + 1], ors2[:, 2 + sl:3 + sl],
                ALU.max), tick="dve")
            eb_m = pb[4 + sl]
            OP("tensor", lambda e, eb_m=eb_m, sl=sl: e.matmul(
                eb_m[0:1, 128:256], mcol[:, sl:sl + 1], s_idm8[:],
                start=True, stop=True, skip_group_check=True),
               waits=[t_mc, c_im8, r_crow.wait(it)],
               tick="pe")
            t_mt = S_.count("pe")
            t_cred = OP("vector", lambda e, eb_m=eb_m, sl=sl: e.tensor_reduce(
                crow[0:1, sl:sl + 1], eb_m[0:1, 128:256], AX.X, ALU.min),
                waits=[t_mt], tick="dve")
            r_mcol.done(it, t_cred)
            OP("tensor", lambda e, eb_m=eb_m, sl=sl: e.matmul(
                eb_m[:, 256:257], s_ones[0:1, :], crow[0:1, sl:sl + 1],
                start=True, stop=True, skip_group_check=True),
                waits=[t_cred], tick="pe")
            t_bb = S_.count("pe")
            r_crow.done(it, t_bb)
            t_bc = OP("scalar", lambda e, eb_m=eb_m, sl=sl: e.activation(
                biasc[:, sl:sl + 1], eb_m[:, 256:257], AF.Copy, scale=1.0),
                waits=[t_bb, r_biasc.wait(it)], tick="act")
            OP("scalar", lambda e, e0=e0, sl=sl: e.activation(
                ptl[sl][:, 0:512], e0[:, 0:512], AF.Exp, scale=8.0,
                bias=biasc[:, sl:sl + 1]),
               waits=[t_bc, r_ptl.wait(it)], tick="act")
            t_exp = OP("scalar", lambda e, e1=e1, sl=sl: e.activation(
                ptl[sl][:, 512:640], e1[:, 0:128], AF.Exp, scale=8.0,
                bias=biasc[:, sl:sl + 1]), tick="act")
            exp_tick[it] = t_exp
            st2[it]["exp"] = t_exp
            r_el.done(it, t_exp)
            r_biasc.done(it, t_exp)

        def c2_sb(it):
            h, hp, bs, qt = its[it]
            sl = it % 2
            ob = 4 + sl
            g0 = qt + qoff - 4
            t_exp = st2[it]["exp"]
            for gi in range(5):
                g = g0 + gi
                OP("tensor", lambda e, g=g, gi=gi, sl=sl, h=h, ob=ob: e.matmul(
                    pb[ob][:, 0:65],
                    ptl[sl][:, gi * 128:gi * 128 + 128],
                    vaug[:, g * 520 + h * 65:g * 520 + h * 65 + 65],
                    start=(gi == 0), stop=(gi == 4), skip_group_check=True),
                   waits=([t_exp, r_ol.wait(it)] if gi == 0 else []),
                   tick=("pe" if gi == 4 else None))
            t_pv = S_.count("pe")
            r_ptl.done(it, t_pv)
            OP("vector", lambda e, ob=ob: e.tensor_scalar(
                ors2[:, 0:1], pb[ob][:, 64:65], 1e-30, None, ALU.max, ALU.bypass),
               waits=[t_pv, r_ors.wait(it)], tick="dve")
            t_rc = OP("vector", lambda e: e.reciprocal(ors2[:, 1:2], ors2[:, 0:1]),
                      tick="dve")
            t_on = OP("scalar", lambda e, ob=ob, sl=sl: e.activation(
                otm2[sl][:], pb[ob][:, 0:64], AF.Copy, scale=ors2[:, 1:2]),
                waits=[t_rc, r_otm2.wait(it)], tick="act")
            r_ol.done(it, t_on)
            r_ors.done(it, t_on)
            OP("tensor", lambda e, bs=bs, sl=sl: e.transpose(
                pbt[sl][bs:bs + 64, 0:128], otm2[sl][:], s_id_bf[:]),
               waits=[t_on, r_pbtL.wait(it)], tick="pe")
            t_otr = S_.count("pe")
            r_otm2.done(it, t_otr)
            t_oc = OP("scalar", lambda e, bs=bs, hp=hp, qt=qt, sl=sl: e.activation(
                ofv2[bs:bs + 64, hp, qt * 128:(qt + 1) * 128],
                pbt[sl][bs:bs + 64, 0:128], AF.Copy, scale=1.0),
                waits=[t_otr], tick="act")
            r_pbtL.done(it, t_oc)

        for slot in range(N2 + 2):
            if slot < N2:
                c2_s0(slot)
            if 1 <= slot <= N2:
                c2_sa(slot - 1)
            if slot >= 2:
                c2_sb(slot - 2)
        if "TAP2" in DEV_SKIP and l == 0:
            OP("sync", lambda e: e.dma_start(tap_o[:, 0:4 * To * 128], ofm[:]),
               waits=[S_.count("dve"), S_.count("pe"), S_.count("act")], tick="d_sp")
        S_.barrier(ENG)
        ar.reset(mKV)

        afm = ar.alloc("afm", [128, 8 * AST], BF16)
        mC3 = ar.mark()
        wo_b = ar.alloc("wo_b", [128, 4 * D], BF16)
        for ci in range(4):
            load(wo_b[:, ci * D:(ci + 1) * D], dr[f"wo{l}"][ci * 128:(ci + 1) * 128, :],
                 q="gpsimd")
        t_wob = S_.count("d_gp")
        avv = afm.ap().rearrange("p (c t) -> p c t", c=8, t=AST)
        xin_stg = [ar.alloc(f"xinstg{i}", [128, D], F32) for i in range(2)]
        a_stg = [ar.alloc(f"astg{i}", [128, D], F32) for i in range(2)]
        ab_stg = [ar.alloc(f"abstg{i}", [128, D], BF16) for i in range(2)]
        stt4 = [ar.alloc(f"stt4_{i}", [128, 4], F32) for i in range(2)]
        srow = [ar.alloc(f"srow{i}", [1, To * 128 + 8], F32) for i in range(4)]
        OP("vector", lambda e: e.memset(avv[:, :, 0:1], 0.0), tick="dve")
        for i in range(4):
            OP("vector", lambda e, i=i: e.memset(srow[i][0:1, :], 0.0), tick="dve")
        t_sr0 = S_.count("dve")

        if "TAP2" in DEV_SKIP and l == 0:
            OP("sync", lambda e: e.dma_start(tap_wo[0], wo_b[:]),
               waits=[t_wob], tick="d_sp")
        r_c3bank = Ring(4)
        r_astg = Ring(2)
        r_xinstg = Ring(2)
        r_abstg = Ring(2)
        r_stt4 = Ring(2)
        r_pbtD = Ring(2)
        for ti in range(To):
            sl = ti % 2
            t_xi = OP("sync", lambda e, ti=ti, sl=sl: e.dma_start(
                xin_stg[sl][:],
                in_scr[(ti + in_off) * 128:(ti + in_off) * 128 + 128, :]),
                waits=[r_xinstg.wait(ti), scr_ready], tick="d_sp")
            for half in range(2):
                bank = sl * 2 + half
                for ci in range(4):
                    OP("tensor", lambda e, ci=ci, half=half, ti=ti, bank=bank: e.matmul(
                        pb[bank][:, 0:512], ofv2[:, ci, ti * 128:(ti + 1) * 128],
                        wo_b[:, ci * D + half * 512:ci * D + half * 512 + 512],
                        start=(ci == 0), stop=(ci == 3)),
                       waits=([t_wob, r_c3bank.wait(bank)] if ci == 0 else []),
                       tick=("pe" if ci == 3 else None))
            t_mm = S_.count("pe")
            t_a = None
            for half in range(2):
                bank = sl * 2 + half
                hs = slice(half * 512, half * 512 + 512)
                t_a = OP("vector", lambda e, bank=bank, hs=hs, sl=sl: e.tensor_tensor(
                    a_stg[sl][:, hs], pb[bank][:, 0:512], xin_stg[sl][:, hs], ALU.add),
                    waits=[t_mm, t_xi, r_astg.wait(ti)], tick="dve")
                r_c3bank.done(bank, t_a)
            r_xinstg.done(ti, t_a)
            t_adma = OP("sync", lambda e, ti=ti, sl=sl: e.dma_start(
                a_scr[ti * 128:(ti + 1) * 128, :], a_stg[sl][:]),
               waits=[t_a], tick="d_sp")
            r_astg.done(ti, t_adma)
            s4 = stt4[sl]
            OP("vector", lambda e, sl=sl, s4=s4: e.tensor_scalar(
                dmp[:, 0:512], a_stg[sl][:, 0:512], 1.0, None, ALU.mult, ALU.add,
                accum_out=s4[:, 0:1]), waits=[t_a, r_stt4.wait(ti)], tick="dve")
            OP("vector", lambda e, sl=sl, s4=s4: e.tensor_scalar(
                dmp[:, 0:512], a_stg[sl][:, 512:1024], 1.0, None, ALU.mult, ALU.add,
                accum_out=s4[:, 1:2]), tick="dve")
            OP("scalar", lambda e, sl=sl, s4=s4: e.activation(
                ab_stg[sl][:, 0:512], a_stg[sl][:, 0:512], AF.Square,
                accum_out=s4[:, 2:3]), waits=[t_a, r_abstg.wait(ti)], tick="act")
            t_s4 = OP("scalar", lambda e, sl=sl, s4=s4: e.activation(
                ab_stg[sl][:, 512:1024], a_stg[sl][:, 512:1024], AF.Square,
                accum_out=s4[:, 3:4]), tick="act")
            t_ab = OP("scalar", lambda e, sl=sl: e.copy(ab_stg[sl][:], a_stg[sl][:]),
                      waits=[t_a, r_abstg.wait(ti)], tick="act")
            r_astg.done(ti, t_ab)
            pt = pbt[sl]
            for c in range(8):
                OP("tensor", lambda e, c=c, sl=sl, pt=pt: e.transpose(
                    pt[:, c * 128:(c + 1) * 128], ab_stg[sl][:, c * 128:(c + 1) * 128],
                    s_id_bf[:]),
                   waits=([t_ab, r_pbtD.wait(ti)] if c == 0 else []),
                   tick=("pe" if c == 7 else None))
            t_tra = S_.count("pe")
            r_abstg.done(ti, t_tra)
            t_cpa = OP("vector", lambda e, ti=ti, pt=pt: e.tensor_copy(
                avv[:, :, 1 + ti * 128:1 + (ti + 1) * 128],
                pt[:, 0:1024].rearrange("p (c t) -> p c t", c=8)),
               waits=[t_tra], tick="dve")
            r_pbtD.done(ti, t_cpa)
            for k in range(4):
                OP("tensor", lambda e, k=k, s4=s4: e.transpose(
                    pb[4 + (0 if True else 0)][0:1, 0:128] if False else
                    pb[4][0:1, 0:128],
                    s4[:, k:k + 1], s_id_f[:]),
                   waits=[t_s4, c_idf, r_c3bank.wait(4) if False else None,
                          S_.count("dve")],
                   tick="pe")
                t_trs = S_.count("pe")
                OP("vector", lambda e, k=k, ti=ti: e.tensor_copy(
                    srow[k][0:1, 8 + ti * 128:8 + (ti + 1) * 128], pb[4][0:1, 0:128]),
                   waits=[t_trs, t_sr0], tick="dve")
            t_done4 = S_.count("dve")
            r_stt4.done(ti, t_done4)
        S_.barrier(ENG)

        murow = ar.alloc("murow", [1, To * 128], F32)
        varow = ar.alloc("varow", [1, To * 128], F32)
        R = To * 128
        OP("vector", lambda e: e.tensor_tensor(
            murow[0:1, :], srow[0][0:1, 8:8 + R], srow[1][0:1, 7:7 + R], ALU.add),
           tick="dve")
        OP("vector", lambda e: e.tensor_scalar(
            murow[0:1, :], murow[0:1, :], 1.0 / D, None, ALU.mult, ALU.bypass),
           tick="dve")
        OP("vector", lambda e: e.tensor_tensor(
            varow[0:1, :], srow[2][0:1, 8:8 + R], srow[3][0:1, 7:7 + R], ALU.add),
           tick="dve")
        OP("vector", lambda e: e.tensor_scalar(
            varow[0:1, :], varow[0:1, :], 1.0 / D, None, ALU.mult, ALU.bypass),
           tick="dve")
        OP("vector", lambda e: e.tensor_tensor(
            srow[0][0:1, 8:8 + R], murow[0:1, :], murow[0:1, :], ALU.mult),
           tick="dve")
        OP("vector", lambda e: e.tensor_tensor(
            varow[0:1, :], varow[0:1, :], srow[0][0:1, 8:8 + R], ALU.subtract),
           tick="dve")
        t_vv = OP("vector", lambda e: e.tensor_scalar(
            varow[0:1, :], varow[0:1, :], 0.0, 1e-5, ALU.max, ALU.add), tick="dve")
        t_sv = OP("scalar", lambda e: e.activation(
            varow[0:1, :], varow[0:1, :], AF.Sqrt), waits=[t_vv], tick="act")
        OP("vector", lambda e: e.reciprocal(varow[0:1, :], varow[0:1, :]),
           waits=[t_sv], tick="dve")
        OP("vector", lambda e: e.tensor_copy(arow[0:1, :], varow[0:1, :]), tick="dve")
        OP("vector", lambda e: e.tensor_tensor(
            murow[0:1, :], murow[0:1, :], varow[0:1, :], ALU.mult), tick="dve")
        t_ab2 = OP("vector", lambda e: e.tensor_scalar(
            brow[0:1, :], murow[0:1, :], -1.0, None, ALU.mult, ALU.bypass), tick="dve")

        if "TAP2" in DEV_SKIP and l == 0:
            OP("sync", lambda e: e.dma_start(tap_wo[1], wo_b[:]),
               waits=[S_.count("dve"), S_.count("pe"), S_.count("act")], tick="d_sp")
        S_.barrier(ENG)
        ar.reset(mC3)

        if out_kind == "x1" and xinfm_l1 is None:
            topY.append(ar.mark_top())
            xinfm_l1 = ar.alloc_top("xinfm_l1", [128, 8 * AST], BF16)
            OP("vector", lambda e: e.memset(
                xinfm_l1.ap().rearrange("p (c t) -> p c t", c=8, t=AST)[:, :, 0:1],
                0.0), tick="dve")
        mC4 = ar.mark()
        gam = ar.alloc("gam", [1, D], BF16)
        t_g = load(gam[0:1, :], dr[f"lng_row{l}"][:], q="sync")
        lnb = ar.alloc("lnb", [128, 8], F32)
        t_lb = load(lnb[:], dr[f"lnb_col{l}"][:], q="sync")
        b1c = ar.alloc("b1c", [128, F1 // 128], F32)
        t_b1 = load(b1c[:], dr[f"b1p{l}"][:], q="sync")
        w1buf = ar.alloc("w1buf", [128, 8 * 6 * 256], BF16)
        w2buf = ar.alloc("w2buf", [128, 6 * D], BF16)
        hgate = [ar.alloc(f"hgate{i}", [128, 6 * 512], BF16) for i in range(2)]
        ggel = [ar.alloc(f"ggel{i}", [128, 512], BF16) for i in range(2)]
        x2stg = [ar.alloc(f"x2stg{i}", [128, D], F32) for i in range(2)]
        ngroups = (To * 128 + 511) // 512
        wq_guard = [None]

        r_hbk = Ring(4)
        r_x2bk = Ring(2)
        r_ggel = Ring(2)
        r_hg = Ring(2)
        r_x2stg = Ring(2)
        r_gbk = Ring(2)

        for grp in range(ngroups):
            t0 = grp * 512
            gn = min(512, To * 128 - t0)
            for ci in range(8):
                OP("tensor", lambda e, ci=ci, t0=t0, gn=gn: e.matmul(
                    pb[4][:, 0:gn], gam[0:1, ci * 128:(ci + 1) * 128],
                    arow[0:1, t0:t0 + gn], start=True, stop=True),
                   waits=[t_ab2, t_g, r_gbk.wait(0)], tick=None)
                OP("tensor", lambda e, ci=ci, t0=t0, gn=gn: e.matmul(
                    pb[5][:, 0:gn], gam[0:1, ci * 128:(ci + 1) * 128],
                    brow[0:1, t0:t0 + gn], start=True, stop=True),
                   waits=[r_gbk.wait(1)], tick="pe")
                t_gb = S_.count("pe")
                co = 1 + t0 - (0 if ci < 4 else 1)
                OP("vector", lambda e, ci=ci, co=co, gn=gn: e.tensor_tensor(
                    dmp[:, 0:gn], avv[:, ci, co:co + gn], pb[4][:, 0:gn], ALU.mult),
                   waits=[t_gb], tick="dve")
                t_hh = OP("vector", lambda e, ci=ci, co=co, gn=gn:
                          e.scalar_tensor_tensor(
                    avv[:, ci, co:co + gn], dmp[:, 0:gn], lnb[:, ci:ci + 1],
                    pb[5][:, 0:gn], ALU.add, ALU.add), waits=[t_lb], tick="dve")
                r_gbk.done(0, t_hh)
                r_gbk.done(1, t_hh)
        t_hall = S_.count("dve")

        for halfp, (fp0, fpn) in enumerate([(0, 6), (6, 5), (11, 6), (17, 5)]):
            w_free = wq_guard[0]
            for ci in range(8):
                OP("gpsimd", lambda e, ci=ci, fp0=fp0, fpn=fpn: e.dma_start(
                    w1buf[:, ci * 6 * 256:ci * 6 * 256 + fpn * 256],
                    dr[f"w1p{l}"][ci * 128:(ci + 1) * 128,
                                  fp0 * 256:(fp0 + fpn) * 256]),
                   waits=[w_free], tick="d_gp")
            t_w1 = S_.count("d_gp")
            for hc in range(fpn):
                OP("gpsimd", lambda e, hc=hc, fp0=fp0: e.dma_start(
                    w2buf[:, hc * D:(hc + 1) * D],
                    dr[f"w2p{l}"][(fp0 + hc) * 128:(fp0 + hc + 1) * 128, :]),
                   waits=[w_free], tick="d_gp")
            t_w2 = S_.count("d_gp")
            for grp in range(ngroups):
                t0 = grp * 512
                gn = min(512, To * 128 - t0)
                gtiles = gn // 128
                g2 = halfp * ngroups + grp
                hg = hgate[g2 % 2]
                for fp in range(fpn):
                    fpg = fp0 + fp
                    slb = fp % 2
                    gx = g2 * 6 + fp
                    for hag in range(2):
                        bank = slb * 2 + hag
                        fcol = 2 * fp * 128 + hag * 128
                        for ci in range(8):
                            co = 1 + t0 - (0 if ci < 4 else 1)
                            OP("tensor", lambda e, ci=ci, fcol=fcol, bank=bank, gn=gn,
                               co=co: e.matmul(
                                pb[bank][:, 0:gn],
                                w1buf[:, ci * 6 * 256 + fcol:
                                      ci * 6 * 256 + fcol + 128],
                                avv[:, ci, co:co + gn], start=(ci == 0), stop=(ci == 7)),
                               waits=([t_w1, t_hall, r_hbk.wait(bank)] if ci == 0 else []),
                               tick=("pe" if ci == 7 else None))
                    t_h1 = S_.count("pe")
                    OP("scalar", lambda e, slb=slb, fpg=fpg, gn=gn, gx=gx: e.activation(
                        ggel[gx % 2][:, 0:gn], pb[slb * 2 + 1][:, 0:gn], AF.Gelu,
                        bias=b1c[:, 2 * fpg + 1:2 * fpg + 2]),
                       waits=[t_h1, t_b1, r_ggel.wait(gx)], tick="act")
                    t_gl = S_.count("act")
                    t_hgt = OP("vector", lambda e, slb=slb, fp=fp, fpg=fpg, gn=gn,
                               gx=gx, hg=hg: e.scalar_tensor_tensor(
                                hg[:, fp * 512:fp * 512 + gn], pb[slb * 2][:, 0:gn],
                                b1c[:, 2 * fpg:2 * fpg + 1], ggel[gx % 2][:, 0:gn],
                                ALU.add, ALU.mult),
                               waits=[t_gl, r_hg.wait(g2)], tick="dve")
                    r_ggel.done(gx, t_hgt)
                    r_hbk.done(slb * 2, t_hgt)
                    r_hbk.done(slb * 2 + 1, t_hgt)
                t_hgall = S_.count("dve")
                for tt in range(gtiles):
                    ti = grp * 4 + tt
                    for halfd in range(2):
                        bank = 4 + halfd
                        for hc in range(fpn):
                            OP("tensor", lambda e, hc=hc, tt=tt, halfd=halfd, bank=bank,
                               fpn=fpn, hg=hg: e.matmul(
                                pb[bank][:, 0:512],
                                hg[:, hc * 512 + tt * 128:hc * 512 + tt * 128 + 128],
                                w2buf[:, hc * D + halfd * 512:
                                      hc * D + halfd * 512 + 512],
                                start=(hc == 0), stop=(hc == fpn - 1)),
                               waits=([t_hgall, t_w2, r_x2bk.wait(halfd)]
                                      if hc == 0 else []),
                               tick=("pe" if hc == fpn - 1 else None))
                    t_x2 = S_.count("pe")
                    sl = tt % 2
                    OP("vector", lambda e, sl=sl: e.tensor_copy(
                        x2stg[sl][:, 0:512], pb[4][:, 0:512]),
                       waits=[t_x2, r_x2stg.wait(tt)], tick="dve")
                    t_xc = OP("vector", lambda e, sl=sl: e.tensor_copy(
                        x2stg[sl][:, 512:1024], pb[5][:, 0:512]), tick="dve")
                    r_x2bk.done(0, t_xc)
                    r_x2bk.done(1, t_xc)
                    if halfp == 0:
                        t_dx = OP("gpsimd", lambda e, ti=ti, sl=sl: e.dma_start(
                            x2_scr[ti * 128:(ti + 1) * 128, :], x2stg[sl][:]),
                            waits=[t_xc], tick="d_gp")
                    else:
                        t_dx = OP("gpsimd", lambda e, ti=ti, sl=sl: e.dma_start(
                            x2_scr[ti * 128:(ti + 1) * 128, :], x2stg[sl][:],
                            accum_op=ALU.add), waits=[t_xc], tick="d_gp")
                    r_x2stg.done(tt, t_dx)
                r_hg.done(g2, S_.count("pe"))
            wq_guard[0] = S_.count("pe")
        t_x2_all = S_.count("d_gp")
        S_.barrier(ENG)
        ar.reset(mC4)

        b2bc = ar.alloc("b2bc", [128, D], F32)
        b2row = ar.alloc("b2row", [1, D], BF16)
        t_b2r = load(b2row[0:1, :], dr[f"b2_{l}"][:], q="gpsimd")
        t_b2bc = None
        for halfd in range(2):
            t_b2bc = bcast_row_to_tile(b2row[0:1, halfd * 512:halfd * 512 + 512],
                                       b2bc[:, halfd * 512:halfd * 512 + 512], 512,
                                       [t_b2r], 4)
        xin2_stg = [ar.alloc(f"xinstg2_{i}", [128, D], F32) for i in range(2)]
        xo_stg = [ar.alloc(f"xostg{i}", [128, D], F32) for i in range(2)]
        x2l_stg = [ar.alloc(f"x2lstg{i}", [128, D], F32) for i in range(2)]
        xob_stg = [ar.alloc(f"xobstg{i}", [128, D], BF16) for i in range(2)]
        opstg2 = [ar.alloc(f"opstg2_{i}", [128, D], F32) for i in range(2)]
        xl1v = (xinfm_l1.ap().rearrange("p (c t) -> p c t", c=8, t=AST)
                if xinfm_l1 is not None else None)
        r_xo = Ring(2)
        r_xob = Ring(2)
        r_op2 = Ring(2)
        r_pbtE = Ring(2)
        r_xin2 = Ring(2)
        r_x2l = Ring(2)
        for ti in range(To):
            sl = ti % 2
            t_a2 = OP("sync", lambda e, ti=ti, sl=sl: e.dma_start(
                xin2_stg[sl][:], a_scr[ti * 128:(ti + 1) * 128, :]),
                waits=[r_xin2.wait(ti)], tick="d_sp")
            t_x2l = OP("sync", lambda e, ti=ti, sl=sl: e.dma_start(
                x2l_stg[sl][:], x2_scr[ti * 128:(ti + 1) * 128, :]),
                waits=[t_x2_all, r_x2l.wait(ti)], tick="d_sp")
            OP("vector", lambda e, sl=sl: e.tensor_tensor(
                xo_stg[sl][:], x2l_stg[sl][:], b2bc[:], ALU.add),
               waits=[t_x2l, t_b2bc, r_xo.wait(ti)], tick="dve")
            t_add = OP("vector", lambda e, sl=sl: e.tensor_tensor(
                xo_stg[sl][:], xo_stg[sl][:], xin2_stg[sl][:], ALU.add),
               waits=[t_a2], tick="dve")
            r_xin2.done(ti, t_add)
            r_x2l.done(ti, t_add)
            vcol = ti + (T2 - To)
            t_xo = OP("vector", lambda e, sl=sl, vcol=vcol: e.tensor_scalar(
                xo_stg[sl][:], xo_stg[sl][:], s_vld[:, vcol:vcol + 1], None,
                ALU.mult, ALU.bypass), tick="dve")
            if out_kind == "x1":
                t_d1 = OP("sync", lambda e, ti=ti, sl=sl: e.dma_start(
                    x1_scr[ti * 128:(ti + 1) * 128, :], xo_stg[sl][:]),
                    waits=[t_xo], tick="d_sp")
                t_xb = OP("scalar", lambda e, sl=sl: e.copy(
                    xob_stg[sl][:], xo_stg[sl][:]),
                    waits=[t_xo, r_xob.wait(ti)], tick="act")
                r_xo.done(ti, t_xb)
                r_xo.done(ti, t_d1)
                pt = pbt[sl]
                for c in range(8):
                    OP("tensor", lambda e, c=c, sl=sl, pt=pt: e.transpose(
                        pt[:, c * 128:(c + 1) * 128],
                        xob_stg[sl][:, c * 128:(c + 1) * 128], s_id_bf[:]),
                       waits=([t_xb, r_pbtE.wait(ti)] if c == 0 else []),
                       tick=("pe" if c == 7 else None))
                t_tr3 = S_.count("pe")
                r_xob.done(ti, t_tr3)
                t_cp3 = OP("vector", lambda e, ti=ti, pt=pt: e.tensor_copy(
                    xl1v[:, :, 1 + ti * 128:1 + (ti + 1) * 128],
                    pt[:, 0:1024].rearrange("p (c t) -> p c t", c=8)),
                   waits=[t_tr3], tick="dve")
                r_pbtE.done(ti, t_cp3)
            else:
                if ti >= 1:
                    t_op = OP("gpsimd", lambda e, ti=ti, sl=sl: e.dma_start(
                        opstg2[sl][:], outp_scr[(ti - 1) * 128:ti * 128, :]),
                        waits=[r_op2.wait(ti), outp_done[0]], tick="d_gp")
                    t_f = OP("vector", lambda e, sl=sl: e.tensor_tensor(
                        xo_stg[sl][:], xo_stg[sl][:], opstg2[sl][:], ALU.add),
                       waits=[t_op, t_xo, r_op2.wait(ti)], tick="dve")
                    t_do = OP("sync", lambda e, ti=ti, sl=sl: e.dma_start(
                        out[(ti - 1) * 128:ti * 128, :], xo_stg[sl][:]),
                        waits=[t_f], tick="d_sp")
                    r_xo.done(ti, t_do)
                    r_op2.done(ti, t_do)
                else:
                    r_xo.done(ti, t_xo)
        S_.barrier(ENG)
        ar.reset(m0)

    topY = []
    ffn_layer(0, T2, T1, qoff=5, xin_st=XST,
              in_scr=xn_scr, in_off=5, out_kind="x1", scr_ready=a_dma_done)
    if "L1" not in DEV_SKIP:
        ffn_layer(1, T1, T0E, qoff=4, xin_st=AST,
                  in_scr=x1_scr, in_off=4, out_kind="final", scr_ready=None)

    last = S_.count("d_sp")
    S_.op("sync", "NOP", [last], "fin")

    ar.finalize()
    with nc.Block() as blk:
        S_.replay(blk, ENG)
    global LAST_SCHED
    LAST_SCHED = S_
    return nc



LAST_SCHED = None
_CACHE = {}


def kernel(**inputs):
    x = np.asarray(inputs["L0_attn_x"], np.float32)
    shared = _prep_shared(inputs)
    in_maps = []
    for c in range(NCORE):
        m = dict(shared)
        m.update(_prep_core(x, c // 4, c % 4))
        in_maps.append(m)
    if "nc" not in _CACHE:
        _CACHE["nc"] = build_program()
    res = run_bass_kernel_spmd(_CACHE["nc"], in_maps, core_ids=list(range(NCORE)))
    parts = [np.asarray(r["out"], np.float32) for r in res.results]
    return np.stack([np.concatenate(parts[b * 4:(b + 1) * 4], axis=0)
                     for b in range(B)], axis=0)


if __name__ == "__main__":
    nc = build_program()
    n = sum(1 for _ in nc.all_instructions())
    print("built ok, instructions:", n)
